# version 03
# baseline (speedup 1.0000x reference)
import math
import numpy as np
import ml_dtypes
import concourse.bass as bass
import concourse.mybir as mybir
from concourse.bass_utils import run_bass_kernel_spmd

F32 = mybir.dt.float32
BF16 = mybir.dt.bfloat16
F16 = mybir.dt.float16
U8 = mybir.dt.uint8
AF = mybir.ActivationFunctionType
ALU = mybir.AluOpType
AX = mybir.AxisListType

D = 4096
T = 2048
NCORES = 8
KT = D // 128
TB = 4
TW = 512
HEADS = 32
DK = 128
INTER = 11008
IKT = INTER // 128
DM = 128
EPS_LN = 1e-5
BF = ml_dtypes.bfloat16


def perm_eo(n):
    p = np.arange(n).reshape(-1, 256)
    p = np.concatenate([p[:, 0::2], p[:, 1::2]], axis=1).reshape(-1)
    return p


DEBUG_CONSERVATIVE = False
PH = perm_eo(D)
PI = perm_eo(INTER)


ENGINES = ("sync", "scalar", "vector", "gpsimd", "tensor")


class _MockIns:
    def then_inc(self, *a, **kw):
        return self


class Milestone:
    __slots__ = ("eng", "value", "tag")

    def __init__(self, tag=""):
        self.eng = None
        self.value = None
        self.tag = tag


def mss(n, tag=""):
    return [Milestone(tag) for _ in range(n)]


def mss2(a, b, tag=""):
    return [[Milestone(tag) for _ in range(b)] for _ in range(a)]


class Emitter:

    def __init__(self, ctx, name, real):
        self.ctx = ctx
        self.name = name
        self.real = real

    def __getattr__(self, op):
        if self.real is None:
            def mock(*a, **kw):
                return _MockIns()
            return mock
        return getattr(self.real, op)

    def set(self, ins, ms, k=1):
        c = self.ctx.counts[self.name] + k
        self.ctx.counts[self.name] = c
        if self.real is None:
            ms.eng = self.name
            ms.value = c
        else:
            assert ms.eng == self.name and ms.value == c, \
                f"milestone drift on {self.name}: tag={ms.tag} {ms.value} vs {c}"
            ins.then_inc(self.ctx.semh[self.name], k)
        return ms

    def dchain(self):
        if self.real is not None:
            self.real.drain()

    def setd(self, ms, k=1):
        c = self.ctx.counts[self.name] + k
        self.ctx.counts[self.name] = c
        if self.real is None:
            ms.eng = self.name
            ms.value = c
        else:
            assert ms.eng == self.name and ms.value == c, \
                f"milestone drift on {self.name}: tag={ms.tag} {ms.value} vs {c}"
            self.real.drain().then_inc(self.ctx.semh[self.name], k)
        return ms

    def dma(self, out, in_, ms=None, slot=None):
        key = slot if slot is not None else self.name + ".dma"
        c = self.ctx.counts[key] + 16
        self.ctx.counts[key] = c
        if self.real is None:
            if ms is not None:
                ms.eng = key
                ms.value = c
            return _MockIns()
        ins = self.dma_start(out=out, in_=in_)
        ins.then_inc(self.ctx.semh[key], 16)
        if ms is not None:
            assert ms.eng == key and ms.value == c, \
                f"dma milestone drift {key}: tag={ms.tag} {ms.value} vs {c}"
        return ins

    def wait(self, ms):
        if ms is None or self.real is None:
            return
        assert ms.eng is not None, f"wait on never-set milestone tag={ms.tag}"
        self.real.wait_ge(self.ctx.semh[ms.eng], ms.value)


ALL_SEMS = ENGINES + tuple(e + ".dma" for e in ENGINES)


class Ctx:
    def __init__(self):
        from collections import defaultdict
        self.counts = defaultdict(int)
        self.semh = {}

    def slot(self, name):
        key = "slot." + name
        if key not in self.semh:
            self.semh[key] = self.stack.enter_context(self.nc.semaphore(key))
        return key

    def reset_counts(self):
        self.counts.clear()



def build_program(phase_cfg, io_kinds, n_cores=NCORES):
    nc = bass.Bass("TRN2", target_bir_lowering=False)
    from contextlib import ExitStack
    stack = ExitStack()
    ctx = Ctx()
    ctx.nc = nc
    ctx.stack = stack
    ctx.n_cores = n_cores

    _drams = {}

    def dram(name, shape, dt, kind=None):
        if name in _drams:
            return _drams[name]
        kind = {"in": "ExternalInput", "out": "ExternalOutput"}.get(
            kind or io_kinds.get(name), "Internal")
        h = nc.dram_tensor(name, shape, dt, kind=kind)
        _drams[name] = h
        return h

    ctx.dram = dram
    ctx.sb = lambda name, shape, dt=F32: stack.enter_context(nc.sbuf_tensor(name, shape, dt))
    ctx.ps = lambda name, shape, dt=F32: stack.enter_context(nc.psum_tensor(name, shape, dt))

    with stack:
        for e in ALL_SEMS:
            ctx.semh[e] = stack.enter_context(nc.semaphore("sem_" + e.replace(".", "_")))
        ctx.consts = SharedConsts(ctx)
        ctx.phases = []
        phases = []
        for cls, kw in phase_cfg:
            ph = cls(ctx, **kw)
            phases.append(ph)
            ctx.phases.append(ph)
        ctx.pt_banks = [ctx.ps(f"pt{i}", [128, 1024], BF16) for i in range(2)]
        ctx.sc4 = ctx.ps("sc4", [128, 2048], F32)
        ctx.p_banks = [ctx.sc4[:, i * 512:(i + 1) * 512] for i in range(4)] + \
                      [ctx.ps(f"pb{i}", [128, 512], F32) for i in range(2)]
        ctx.BIG = ctx.sb("poolBIG", [128, KT * T], BF16)
        ctx.MIDA = ctx.sb("poolMIDA", [128, 16384], BF16)
        ctx.MIDF = ctx.sb("poolMIDF", [128, 8192], F32)
        for ph in phases:
            ph.alloc()
        mocks = {e: Emitter(ctx, e, None) for e in ENGINES}
        ctx.reset_counts()
        for e in ENGINES:
            ctx.consts.emit(mocks[e], e)
            for ph in phases:
                getattr(ph, e)(mocks[e])
        ctx.reset_counts()
        block = stack.enter_context(nc.Block())

        def mk(engine_name):
            def body(eng):
                E = Emitter(ctx, engine_name, eng)
                ctx.consts.emit(E, engine_name)
                for ph in phases:
                    getattr(ph, engine_name)(E)
            return body
        block.sync(mk("sync"))
        block.scalar(mk("scalar"))
        block.vector(mk("vector"))
        block.gpsimd(mk("gpsimd"))
        block.tensor(mk("tensor"))
    return nc


class SharedConsts:
    def __init__(self, ctx):
        self.ctx = ctx
        d = lambda n, s, t: ctx.dram(n, s, t, kind="in")
        self.ones_bf_d = d("c_ones_bf", [128, 128], BF16)
        self.ones_f_d = d("c_ones_f", [128, 128], F32)
        self.ident_d = d("c_ident", [128, 128], BF16)
        self.identf_d = d("c_identf", [128, 128], F32)
        self.ident16_d = d("c_ident16", [128, 128], F16)
        self.zeros_d = d("c_zeros", [128, 4], F32)
        self.zbf_d = d("c_zbf", [128, 2], BF16)
        self.f16_d = d("c_f16", [128, 2], F16)
        self.u8_d = d("c_u8", [128, 2], U8)
        self.ones_bf = ctx.sb("ones_bf", [128, 128], BF16)
        self.ones_f = ctx.sb("ones_f", [128, 128], F32)
        self.ident = ctx.sb("ident", [128, 128], BF16)
        self.ident_f = ctx.sb("ident_f", [128, 128], F32)
        self.ident16 = ctx.sb("ident16", [128, 128], F16)
        self.zeros = ctx.sb("c_zeros_s", [128, 4], F32)
        self.zbf = ctx.sb("c_zbf_s", [128, 2], BF16)
        self.cf16 = ctx.sb("c_f16_s", [128, 2], F16)
        self.cu8 = ctx.sb("c_u8_s", [128, 2], U8)
        self.loaded = Milestone("consts")
        ca = ctx.nc.const_aps
        ca.register(F32, 0.0, self.zeros[:, 0:1])
        ca.register(F32, 1.0, self.ones_f[:, 0:1])
        ca.register(BF16, 0.0, self.zbf[:, 0:1])
        ca.register(BF16, 1.0, self.zbf[:, 1:2])
        ca.register(F16, 0.0, self.cf16[:, 0:1])
        ca.register(F16, 1.0, self.cf16[:, 1:2])
        ca.register(U8, 0, self.cu8[:, 0:1])
        ca.register(U8, 127, self.cu8[:, 1:2])
        self.slot = ctx.slot("consts")

    def emit(self, E, eng):
        if eng != "sync":
            E.wait(self.loaded)
            return
        E.dma(self.ones_bf[:], self.ones_bf_d.ap(), slot=self.slot)
        E.dma(self.ones_f[:], self.ones_f_d.ap(), slot=self.slot)
        E.dma(self.ident[:], self.ident_d.ap(), slot=self.slot)
        E.dma(self.ident16[:], self.ident16_d.ap(), slot=self.slot)
        E.dma(self.zeros[:], self.zeros_d.ap(), slot=self.slot)
        E.dma(self.zbf[:], self.zbf_d.ap(), slot=self.slot)
        E.dma(self.cf16[:], self.f16_d.ap(), slot=self.slot)
        E.dma(self.cu8[:], self.u8_d.ap(), slot=self.slot)
        E.dma(self.ident_f[:], self.identf_d.ap(), ms=self.loaded, slot=self.slot)

    @staticmethod
    def host_inputs():
        return {
            "c_ones_bf": np.ones((128, 128), BF),
            "c_ones_f": np.ones((128, 128), np.float32),
            "c_ident": np.eye(128).astype(BF),
            "c_identf": np.eye(128).astype(np.float32),
            "c_ident16": np.eye(128).astype(np.float16),
            "c_zeros": np.zeros((128, 4), np.float32),
            "c_zbf": np.tile(np.array([0, 1], BF), (128, 1)),
            "c_f16": np.tile(np.array([0, 1], np.float16), (128, 1)),
            "c_u8": np.tile(np.array([0, 127], np.uint8), (128, 1)),
        }



class P_LN:

    def __init__(self, ctx, xname="x_tok", src_fm=False, out_dram=None, dbg=False,
                 prefix="ln0", xdram=None, no_norm=False, dump_consts=False,
                 out_f16=False):
        self.f16 = out_f16
        self.ctx = ctx
        self.pref = prefix
        self.src_fm = src_fm
        if xdram is not None:
            self.x_d = xdram
        else:
            self.x_d = ctx.dram(xname, [D, T] if src_fm else [T, D], F32)
        self.out_d = ctx.dram(out_dram, [D, T], F16 if out_f16 else BF16) \
            if out_dram else None
        self.dbg = dbg
        self.no_norm = no_norm
        self.dump_consts = dump_consts
        if dump_consts:
            self.ci_dbg = ctx.dram(prefix + "_ci_dbg", [128, 256], BF16)
            self.cf_dbg = ctx.dram(prefix + "_cf_dbg", [128, 256], F32)
        if dbg:
            self.xn_dbg = ctx.dram(prefix + "_xn_dbg", [D, T], BF16)
            self.xtok_dbg = ctx.dram(prefix + "_xtok_dbg", [T, D], BF16)
            self.st_dbg = ctx.dram(prefix + "_st_dbg", [T, 8], F32)
            self.sq_dbg = ctx.dram(prefix + "_sq_dbg", [T, D], BF16)
            self.c_dbg = ctx.dram(prefix + "_c_dbg", [128, 8], F32)

    def alloc(self):
        ctx = self.ctx
        p = self.pref
        cv = (lambda ap: ap.bitcast(F16)) if self.f16 else (lambda ap: ap)
        self.xn = cv(ctx.BIG)
        self.xb = [cv(ctx.MIDA[:, 0:D]), cv(ctx.MIDA[:, D:2 * D])]
        self.sq = cv(ctx.MIDA[:, 2 * D:3 * D])
        if self.src_fm:
            self.xf = [ctx.MIDF[:, 0:D]]
            self.tokf = ctx.MIDF[:, D:2 * D]
        else:
            self.xf = [ctx.MIDF[:, 0:D], ctx.MIDF[:, D:2 * D]]
        self.st = ctx.sb(p + "_st", [128, 8], F32)
        self.sts = ctx.sb(p + "_sts", [128, 8], F32)
        self.sl_xf = [ctx.slot(p + "xf0"), ctx.slot(p + "xf1")]
        self.sl_out = ctx.slot(p + "out")
        self.sl_dbg = ctx.slot(p + "dbg")
        self.sl_dbg2 = ctx.slot(p + "dbg2")
        self.ld = mss2(16, 8, p + "ld")
        self.f2t = mss2(16, 8, p + "f2t")
        self.f2te = mss2(16, 8, p + "f2te")
        self.bfc = mss(16, p + "bfc")
        self.sqm = mss(16, p + "sqm")
        self.invm = mss(16, p + "inv")
        self.rstdm = mss(16, p + "rstd")
        self.normm = mss(16, p + "norm")
        self.trm = mss2(8, KT, p + "tr")
        self.evm = mss2(8, KT, p + "ev")
        self.outm = mss2(8, KT, p + "out")
        self.dbg2m = mss(16, p + "dbg2")
        self.dbg3m = mss(16, p + "dbg3")
        self.done = self.evm[7][KT - 1]
        self.dbg_done = Milestone(p + "dbg")

    def xn_ap(self, kt, t0, w):
        return self.xn[:, kt * T + t0: kt * T + t0 + w]

    def xn_ap2(self, kt, tb):
        return self.xn[:, kt * T + tb * 512: kt * T + (tb + 1) * 512]

    def sync(self, E):
        x = self.x_d.ap()
        if self.dump_consts:
            cs = self.ctx.consts
            E.wait(cs.loaded)
            E.dma(self.ci_dbg.ap()[:, 0:128], cs.ident[:], slot=self.sl_dbg)
            E.dma(self.ci_dbg.ap()[:, 128:256], cs.ones_bf[:], slot=self.sl_dbg)
            E.dma(self.cf_dbg.ap()[:, 0:128], cs.ident_f[:], slot=self.sl_dbg)
            E.dma(self.cf_dbg.ap()[:, 128:256], cs.ones_f[:], slot=self.sl_dbg)
        for ti in range(16):
            if self.src_fm:
                if ti > 0:
                    E.wait(self.f2te[ti - 1][7])
                for j in range(8):
                    for a in range(4):
                        d0 = j * 512 + a * 128
                        E.dma(self.xf[0][:, d0:d0 + 128],
                              x[d0:d0 + 128, ti * 128:(ti + 1) * 128],
                              ms=self.ld[ti][j] if (j == 7 and a == 3) else None,
                              slot=self.sl_xf[0])
            else:
                if ti >= 2:
                    E.wait(self.bfc[ti - 2])
                E.dma(self.xf[ti % 2][:], x[ti * 128:(ti + 1) * 128, :], ms=self.ld[ti][7],
                      slot=self.sl_xf[ti % 2])
        if self.out_d is not None:
            for tp in range(8):
                for kt in range(KT):
                    E.wait(self.evm[tp][kt])
                    E.dma(self.out_d.ap()[kt * 128:(kt + 1) * 128, tp * 256:(tp + 1) * 256],
                          self.xn_ap(kt, tp * 256, 256), ms=self.outm[tp][kt], slot=self.sl_out)
            E.wait(self.outm[7][KT - 1])
        if self.dbg:
            E.dma(self.c_dbg.ap()[:, 0:4], self.ctx.consts.zeros[:], slot=self.sl_dbg)
            E.dma(self.c_dbg.ap()[:, 4:8], self.ctx.consts.ones_f[:, 0:4], slot=self.sl_dbg)
            E.wait(self.done)
            for kt in range(KT):
                E.dma(self.xn_dbg.ap()[kt * 128:(kt + 1) * 128, :],
                      self.xn[:, kt * T:(kt + 1) * T],
                      ms=self.dbg_done if kt == KT - 1 else None, slot=self.sl_dbg)
            E.wait(self.dbg_done)

    def scalar(self, E):
        for ti in range(16):
            E.wait(self.invm[ti])
            if self.no_norm:
                E.setd(self.rstdm[ti])
                continue
            E.activation(out=self.sts[:, 0:1], in_=self.st[:, 5:6], func=AF.Ln)
            E.dchain()
            E.activation(out=self.sts[:, 4:5], in_=self.sts[:, 0:1], func=AF.Exp, scale=-0.5)
            E.setd(self.rstdm[ti])
            if self.dbg:
                E.wait(self.normm[ti])
                E.dma(self.st_dbg.ap()[ti * 128:(ti + 1) * 128, :], self.st[:])
                E.dma(self.xtok_dbg.ap()[ti * 128:(ti + 1) * 128, :], self.xb[ti % 2][:],
                      ms=self.dbg2m[ti], slot=self.sl_dbg2)
                E.wait(self.dbg2m[ti])

    def vector(self, E):
        c = self.ctx
        st = self.st
        for tp in range(8):
            for i in range(2):
                ti = tp * 2 + i
                if self.src_fm:
                    for j in range(8):
                        E.wait(self.f2t[ti][j])
                        E.tensor_copy(out=self.tokf[:, j * 512:(j + 1) * 512],
                                      in_=c.p_banks[4 + (j % 2)][:])
                        E.setd(self.f2te[ti][j])
                    src = self.tokf
                else:
                    E.wait(self.ld[ti][7])
                    src = self.xf[ti % 2]
                if ti >= 2:
                    E.wait(self.trm[ti // 2 - 1][KT - 1])
                xb = self.xb[ti % 2]
                E.tensor_copy(out=xb[:], in_=src[:])
                E.dchain()
                if self.no_norm:
                    E.setd(self.normm[ti])
                    E.set(E.engine_nop(), self.bfc[ti])
                    E.setd(self.sqm[ti])
                    E.setd(self.invm[ti])
                    continue
                if ti >= 1:
                    pass
                E.tensor_mul(out=self.sq[:], in0=xb[:], in1=xb[:])
                E.dchain()
                if not self.src_fm:
                    E.set(E.engine_nop(), self.bfc[ti])
                E.reduce_sum(out=st[:, 0:1], in_=xb[:], axis=AX.X)
                E.reduce_sum(out=st[:, 1:2], in_=self.sq[:], axis=AX.X)
                E.dchain()
                E.tensor_scalar(out=st[:, 2:3], in0=st[:, 0:1], scalar1=-1.0 / D, scalar2=None, op0=ALU.mult)
                E.tensor_scalar(out=st[:, 3:4], in0=st[:, 1:2], scalar1=1.0 / D, scalar2=EPS_LN,
                                op0=ALU.mult, op1=ALU.add)
                E.dchain()
                E.tensor_mul(out=st[:, 4:5], in0=st[:, 2:3], in1=st[:, 2:3])
                E.dchain()
                E.tensor_sub(out=st[:, 5:6], in0=st[:, 3:4], in1=st[:, 4:5])
                E.setd(self.invm[ti])
                E.wait(self.rstdm[ti])
                E.tensor_scalar(out=xb[:], in0=xb[:], scalar1=st[:, 2:3], scalar2=self.sts[:, 4:5],
                                op0=ALU.add, op1=ALU.mult)
                E.setd(self.normm[ti])
            for kt in range(KT):
                E.wait(self.trm[tp][kt])
                ptb = c.pt_banks[kt % 2]
                if self.f16:
                    ptb = ptb.bitcast(F16)
                E.tensor_copy(out=self.xn_ap(kt, tp * 256, 256),
                              in_=ptb[:, 0:256])
                E.setd(self.evm[tp][kt])

    def gpsimd(self, E):
        pass

    def tensor(self, E):
        c = self.ctx
        ident = c.consts.ident16 if self.f16 else c.consts.ident
        identf = c.consts.ident_f
        for tp in range(8):
            if tp > 0 and DEBUG_CONSERVATIVE:
                E.wait(self.evm[tp - 1][KT - 1])
            for i in range(2):
                ti = tp * 2 + i
                if self.src_fm:
                    E.wait(self.ld[ti][7])
                    for j in range(8):
                        if ti * 8 + j >= 2:
                            pj = ti * 8 + j - 2
                            E.wait(self.f2te[pj // 8][pj % 8])
                        bank = c.p_banks[4 + (j % 2)]
                        for q in range(4):
                            ins = E.transpose(out=bank[:, q * 128:(q + 1) * 128],
                                              in_=self.xf[0][:, j * 512 + q * 128: j * 512 + (q + 1) * 128],
                                              identity=identf[:])
                        E.set(ins, self.f2t[ti][j])
                E.wait(self.normm[ti])
            for kt in range(KT):
                if kt >= 2:
                    E.wait(self.evm[tp][kt - 2])
                elif tp > 0:
                    E.wait(self.evm[tp - 1][KT - 2 + kt])
                bank = c.pt_banks[kt % 2]
                if self.f16:
                    bank = bank.bitcast(F16)
                for i in range(2):
                    ins = E.transpose(out=bank[:, i * 128:(i + 1) * 128],
                                      in_=self.xb[i][:, kt * 128:(kt + 1) * 128],
                                      identity=ident[:])
                E.set(ins, self.trm[tp][kt])

    @staticmethod
    def host_inputs(core_inputs, b):
        x = np.asarray(core_inputs["L0_attn_x"][b])
        return {"x_tok": np.ascontiguousarray(x[:, PH])}


class P2_QKVG:

    def __init__(self, ctx, ln=None, dbg=False):
        self.ctx = ctx
        self.dbg = dbg
        self.ln = ln if ln is not None else ctx.phases[-1]
        d = ctx.dram
        if dbg:
            self.hdbg_d = d("p2_hdbg", [4, 32, T], BF16)
            self.ydbg_d = d("p2_ydbg", [128, 2048], BF16)
            self.adbg_d = d("p2_adbg", [128, 512], F32)
            self.mdbg_d = d("p2_mdbg", [128, 2048], BF16)
            self.adbg2_d = d("p2_adbg2", [128, 512], F32)
            self.mdbg2_d = d("p2_mdbg2", [128, 2048], BF16)
            self.ydbg2_d = d("p2_ydbg2", [128, 2048], BF16)
            self.dbg00 = Milestone("p2dbg00")
        din = lambda n, s, t: ctx.dram(n, s, t, kind="in")
        self.w_d = [din("wq_b", [32, KT, 128, 128], F32),
                    din("wk_b", [32, KT, 128, 128], F32),
                    din("wv_b", [32, KT, 128, 128], F32),
                    din("w0_b", [32, KT, 128, 128], F32)]
        self.at_d = din("at_b", [KT, 128, 128], F32)
        self.bt_d = din("bt_b", [32, 32, 512], F32)
        self.b0_d = din("b0_t", [128, 32], F32)
        self.m8_d = din("m8_b", [32, TB, 4, 128, 512], BF16)
        self.out_d = [d("qT", [D, T], BF16), d("kT", [D, T], BF16),
                      d("vT", [D, T], BF16), d("g0T", [D, T], F32)]

    def alloc(self):
        ctx = self.ctx
        self.wch = [ctx.sb(f"p2_wch{i}", [128, 128], F32) for i in range(3)]
        self.wbf = [ctx.sb(f"p2_wbf{i}", [128, 128], BF16) for i in range(3)]
        self.hl = [ctx.MIDA[0:32, 8192 + T * i:8192 + T * (i + 1)] for i in range(4)]
        self.btc = [ctx.MIDF[0:32, 3584 + i * 512:3584 + (i + 1) * 512] for i in range(2)]
        self.btb = ctx.sb("p2_btb", [32, 512], BF16)
        self.b0s = ctx.sb("p2_b0s", [128, 32], F32)
        self.ysb = ctx.MIDA[:, 0:2048]
        self.msb = [ctx.MIDA[:, 2048 + i * 2048: 2048 + (i + 1) * 2048] for i in range(2)]
        self.stg = [ctx.MIDA[:, 6144 + i * 512: 6144 + (i + 1) * 512] for i in range(4)]
        self.g0s = [ctx.MIDF[:, i * 512:(i + 1) * 512] for i in range(2)]
        self.tmp = ctx.MIDF[:, 1024:3072]
        self.acc = [ctx.MIDF[:, 3072:3584], ctx.MIDF[:, 4608:5120]]
        self.sl_w = [ctx.slot(f"p2w{i}") for i in range(3)]
        self.sl_bt = [ctx.slot(f"p2bt{i}") for i in range(2)]
        self.sl_b0 = ctx.slot("p2b0")
        self.sl_m = [ctx.slot(f"p2m{i}") for i in range(2)]
        self.sl_st = [ctx.slot(f"p2st{i}") for i in range(4)]
        self.sl_g0 = [ctx.slot(f"p2g0{i}") for i in range(2)]
        self.wld = mss2(4 * 32, KT, "p2wld")
        self.tmm = mss(TB * KT + 4 * 32 * KT, "p2tmm")
        self.hmm = [self.tmm[tb * KT + KT - 1] for tb in range(TB)]
        self.mmog = [self.tmm[TB * KT + gi * KT + KT - 1] for gi in range(4 * 32)]
        self.wcv = mss2(4 * 32, KT, "p2wcv")
        self.evog = mss(4 * 32, "p2ev")
        self.stdma = mss2(3 * 32, TB, "p2st")
        self.ald = mss2(TB, KT, "p2ald")
        self.acv = mss2(TB, KT, "p2acv")
        self.hev = mss(TB, "p2hev")
        self.btld = mss(32, "p2btld")
        self.btcv = mss(32, "p2btcv")
        self.b0ld = Milestone("p2b0ld")
        self.mld = mss2(32, TB, "p2mld")
        self.ymm = [[mss(4, "p2ymm") for _ in range(TB)] for _ in range(32)]
        self.yev = [[mss(4, "p2yev") for _ in range(TB)] for _ in range(32)]
        self.cmb = mss2(32, TB, "p2cmb")
        self.g0rdy = mss2(32, TB, "p2g0rdy")
        self.g0st = mss2(32, TB, "p2g0st")
        self.done = Milestone("p2done")

    def sync(self, E):
        idx = 0
        for tb in range(TB):
            for kt in range(KT):
                if idx >= 3:
                    p = idx - 3
                    E.wait(self.acv[p // KT][p % KT])
                E.dma(self.wch[idx % 3][:], self.at_d.ap()[kt], ms=self.ald[tb][kt],
                      slot=self.sl_w[idx % 3])
                idx += 1
        for mi in range(4):
            for og in range(32):
                gi = mi * 32 + og
                for kt in range(KT):
                    if idx >= 3:
                        pi = idx - 3
                        if pi < TB * KT:
                            E.wait(self.acv[pi // KT][pi % KT])
                        else:
                            pj = pi - TB * KT
                            E.wait(self.wcv[pj // KT][pj % KT])
                    E.dma(self.wch[idx % 3][:], self.w_d[mi].ap()[og, kt],
                          ms=self.wld[gi][kt], slot=self.sl_w[idx % 3])
                    idx += 1

    def scalar(self, E):
        ln = self.ln
        c = self.ctx
        E.wait(ln.done)
        E.dma(self.b0s[:], self.b0_d.ap(), ms=self.b0ld, slot=self.sl_b0)
        idx = 0
        for tb in range(TB):
            for kt in range(KT):
                E.wait(self.ald[tb][kt])
                if idx >= 3:
                    E.wait(self.tmm[idx - 3])
                E.activation(out=self.wbf[idx % 3][:], in_=self.wch[idx % 3][:], func=AF.Copy)
                E.setd(self.acv[tb][kt])
                idx += 1
            E.wait(self.hmm[tb])
            for l in range(4):
                E.activation(out=self.hl[l][:, tb * 512:(tb + 1) * 512],
                             in_=c.p_banks[l][0:32, 0:512], func=AF.Copy)
            E.setd(self.hev[tb])
        for mi in range(4):
            for og in range(32):
                gi = mi * 32 + og
                for kt in range(KT):
                    E.wait(self.wld[gi][kt])
                    E.wait(self.tmm[idx - 3])
                    E.activation(out=self.wbf[idx % 3][:], in_=self.wch[idx % 3][:], func=AF.Copy)
                    E.setd(self.wcv[gi][kt])
                    idx += 1
                if mi < 3:
                    E.wait(self.mmog[gi])
                    for tb in range(TB):
                        if gi >= 1:
                            E.wait(self.stdma[gi - 1][tb])
                        E.activation(out=self.stg[tb][:], in_=c.p_banks[tb][:], func=AF.Copy)
                    E.setd(self.evog[gi])
                else:
                    if og == 0:
                        E.dma(self.btc[0], self.bt_d.ap()[0], ms=self.btld[0],
                              slot=self.sl_bt[0])
                    if og + 1 < 32:
                        if og >= 1:
                            E.wait(self.btcv[og - 1])
                        E.dma(self.btc[(og + 1) % 2], self.bt_d.ap()[og + 1],
                              ms=self.btld[og + 1], slot=self.sl_bt[(og + 1) % 2])
                    E.wait(self.btld[og])
                    E.activation(out=self.btb[:], in_=self.btc[og % 2], func=AF.Copy)
                    E.setd(self.btcv[og])
                    for tb in range(TB):
                        i = og * TB + tb
                        if i >= 1:
                            E.wait(self.cmb[(i - 1) // TB][(i - 1) % TB])
                        for l in range(4):
                            E.wait(self.ymm[og][tb][l])
                            E.activation(out=self.ysb[:, l * 512:(l + 1) * 512],
                                         in_=c.p_banks[4 + (l % 2)][:], func=AF.Copy)
                            E.setd(self.yev[og][tb][l])
                        E.wait(self.cmb[og][tb])
                        if i >= 2:
                            E.wait(self.g0st[(i - 2) // TB][(i - 2) % TB])
                        E.wait(self.b0ld)
                        E.activation(out=self.g0s[i % 2][:], in_=self.acc[i % 2],
                                     func=AF.Silu, bias=self.b0s[:, og:og + 1])
                        E.setd(self.g0rdy[og][tb])
                    E.setd(self.evog[gi])

    def vector(self, E):
        c = self.ctx
        for og in range(32):
            for tb in range(TB):
                E.wait(self.yev[og][tb][3])
                E.wait(self.mld[og][tb])
                y = self.ysb
                m = self.msb[(og * TB + tb) % 2]
                t = self.tmp
                for l in range(4):
                    E.tensor_mul(out=t[:, l * 512:(l + 1) * 512],
                                 in0=m[:, l * 512:(l + 1) * 512],
                                 in1=y[:, l * 512:(l + 1) * 512])
                E.dchain()
                E.tensor_add(out=t[:, 0:512], in0=t[:, 0:512], in1=t[:, 512:1024])
                E.tensor_add(out=t[:, 1024:1536], in0=t[:, 1024:1536], in1=t[:, 1536:2048])
                E.dchain()
                E.tensor_add(out=t[:, 512:1024], in0=t[:, 0:512], in1=t[:, 1024:1536])
                E.dchain()
                i = og * TB + tb
                if i >= 2:
                    E.wait(self.g0rdy[(i - 2) // TB][(i - 2) % TB])
                    if self.dbg and i == 2:
                        E.wait(self.dbg00)
                E.tensor_add(out=self.acc[i % 2], in0=t[:, 512:1024], in1=c.p_banks[tb][:])
                E.setd(self.cmb[og][tb])

    def gpsimd(self, E):
        ln = self.ln
        E.wait(ln.done)
        for i in range(2):
            for l in range(4):
                E.dma(self.msb[i % 2][:, l * 512:(l + 1) * 512],
                      self.m8_d.ap()[0, i, l],
                      ms=self.mld[0][i] if l == 3 else None, slot=self.sl_m[i % 2])
        for mi in range(3):
            for og in range(32):
                gi = mi * 32 + og
                E.wait(self.evog[gi])
                for tb in range(TB):
                    E.dma(self.out_d[mi].ap()[og * 128:(og + 1) * 128, tb * 512:(tb + 1) * 512],
                          self.stg[tb][:], ms=self.stdma[gi][tb], slot=self.sl_st[tb])
        for og in range(32):
            for tb in range(TB):
                i = og * TB + tb
                E.wait(self.g0rdy[og][tb])
                if self.dbg and i == 0:
                    E.dma(self.adbg2_d.ap(), self.acc[0])
                    E.dma(self.mdbg2_d.ap(), self.msb[0])
                    E.dma(self.ydbg2_d.ap(), self.ysb, ms=self.dbg00, slot=self.sl_m[0])
                E.dma(self.out_d[3].ap()[og * 128:(og + 1) * 128, tb * 512:(tb + 1) * 512],
                      self.g0s[i % 2][:], ms=self.g0st[og][tb],
                      slot=self.sl_g0[i % 2])
                j = i + 2
                if j < 32 * TB:
                    jog, jtb = divmod(j, TB)
                    for l in range(4):
                        E.dma(self.msb[j % 2][:, l * 512:(l + 1) * 512],
                              self.m8_d.ap()[jog, jtb, l],
                              ms=self.mld[jog][jtb] if l == 3 else None,
                              slot=self.sl_m[j % 2])
        if self.dbg:
            E.wait(self.hev[TB - 1])
            for l in range(4):
                E.dma(self.hdbg_d.ap()[l], self.hl[l])
            E.wait(self.yev[31][TB - 1][3])
            E.dma(self.ydbg_d.ap(), self.ysb)
            E.wait(self.g0rdy[31][TB - 1])
            E.dma(self.adbg_d.ap(), self.acc[(31 * TB + TB - 1) % 2])
            E.dma(self.mdbg_d.ap(), self.msb[(31 * TB + TB - 1) % 2])
        E.wait(self.g0st[31][TB - 1])
        E.set(E.engine_nop(), self.done)

    def tensor(self, E):
        ln = self.ln
        c = self.ctx
        E.wait(ln.done)
        idx = 0
        ins = None
        for tb in range(TB):
            for kt in range(KT):
                E.wait(self.acv[tb][kt])
                if tb > 0 and kt == 0:
                    E.wait(self.hev[tb - 1])
                for l in range(4):
                    ins = E.matmul(out=c.p_banks[l][0:32, :],
                                   lhsT=self.wbf[idx % 3][:, l * 32:(l + 1) * 32],
                                   rhs=ln.xn_ap2(kt, tb),
                                   start=(kt == 0), stop=(kt == KT - 1))
                E.set(ins, self.tmm[idx])
                idx += 1
        for mi in range(4):
            for og in range(32):
                gi = mi * 32 + og
                for kt in range(KT):
                    E.wait(self.wcv[gi][kt])
                    if kt == 0:
                        E.wait(self.hev[TB - 1] if gi == 0 else self.evog[gi - 1])
                    for tb in range(TB):
                        ins = E.matmul(out=c.p_banks[tb][:],
                                       lhsT=self.wbf[idx % 3][:],
                                       rhs=ln.xn_ap2(kt, tb),
                                       start=(kt == 0), stop=(kt == KT - 1))
                    E.set(ins, self.tmm[idx])
                    idx += 1
                if mi == 3:
                    E.wait(self.btcv[og])
                    for tb in range(TB):
                        E.wait(self.mld[og][tb])
                        for l in range(4):
                            i = (og * TB + tb) * 4 + l
                            if i >= 2:
                                pog, r = divmod(i - 2, TB * 4)
                                ptb, pl = divmod(r, 4)
                                E.wait(self.yev[pog][ptb][pl])
                            ins = E.matmul(out=c.p_banks[4 + (l % 2)][:],
                                           lhsT=self.btb[:, l * 128:(l + 1) * 128],
                                           rhs=self.hl[l][:, tb * 512:(tb + 1) * 512],
                                           start=True, stop=True)
                            E.set(ins, self.ymm[og][tb][l])


class P3_ATTN:

    def __init__(self, ctx, src_names=("qT", "kT", "vT"), out_name="oT"):
        self.ctx = ctx
        self.q_d = ctx.dram(src_names[0], [D, T], BF16)
        self.k_d = ctx.dram(src_names[1], [D, T], BF16)
        self.v_d = ctx.dram(src_names[2], [D, T], BF16)
        self.o_d = ctx.dram(out_name, [D, T], BF16)

    def alloc(self):
        ctx = self.ctx
        B = ctx.BIG
        self.qh = [B[:, 0:T], B[:, T:2 * T]]
        self.kh = [B[:, 2 * T:3 * T], B[:, 3 * T:4 * T]]
        self.vh = [B[:, 4 * T:5 * T], B[:, 5 * T:6 * T]]
        self.vtok = B[:, 6 * T:7 * T]
        self.P = [B[:, 7 * T:8 * T], B[:, 8 * T:9 * T]]
        self.PT = B[:, 9 * T:13 * T]
        self.ostg = [B[:, 13 * T:13 * T + 512], B[:, 13 * T + 512:13 * T + 1024]]
        self.stv = ctx.sb("p3_stv", [128, 8], F32)
        self.sts = ctx.sb("p3_sts", [128, 8], F32)
        self.sc = ctx.sc4
        self.sl_q = [ctx.slot(f"p3q{i}") for i in range(2)]
        self.sl_k = [ctx.slot(f"p3k{i}") for i in range(2)]
        self.sl_v = [ctx.slot(f"p3v{i}") for i in range(2)]
        self.sl_o = [ctx.slot(f"p3o{i}") for i in range(2)]
        H = HEADS
        self.qld = mss(H, "p3qld")
        self.kld = mss(H, "p3kld")
        self.vld = mss(H, "p3vld")
        self.vtr = mss2(H, 4, "p3vtr")
        self.vev = mss2(H, 4, "p3vev")
        self.scm = mss2(H * 4, 4, "p3scm")
        self.mxm = mss2(H * 4, 4, "p3mxm")
        self.expm = mss2(H * 4, 4, "p3expm")
        self.pnm = mss2(H * 4, 4, "p3pnm")
        self.trg = [[mss(4, "p3trg") for _ in range(4)] for _ in range(H * 4)]
        self.ptev = [[mss(4, "p3ptev") for _ in range(4)] for _ in range(H * 4)]
        self.pvm = mss(H * 4, "p3pvm")
        self.oev = mss(H * 4, "p3oev")
        self.ost = mss(H * 4, "p3ost")
        self.done = Milestone("p3done")

    def prev_phase_done(self):
        i = self.ctx.phases.index(self)
        return None if i == 0 else self.ctx.phases[i - 1].done

    def sync(self, E):
        for h in range(HEADS):
            if h == 0:
                E.wait(self.prev_phase_done())
            if h >= 2:
                E.wait(self.pvm[(h - 2) * 4 + 3])
            E.dma(self.qh[h % 2], self.q_d.ap()[h * 128:(h + 1) * 128, :], ms=self.qld[h],
                  slot=self.sl_q[h % 2])
            E.dma(self.kh[h % 2], self.k_d.ap()[h * 128:(h + 1) * 128, :], ms=self.kld[h],
                  slot=self.sl_k[h % 2])
            E.dma(self.vh[h % 2], self.v_d.ap()[h * 128:(h + 1) * 128, :], ms=self.vld[h],
                  slot=self.sl_v[h % 2])

    def tensor(self, E):
        c = self.ctx
        ptuse = 0
        for h in range(HEADS):
            E.wait(self.qld[h])
            E.wait(self.kld[h])
            E.wait(self.vld[h])
            for g in range(4):
                if ptuse >= 2:
                    E.wait(self._ptev_flat(ptuse - 2))
                bank = c.pt_banks[ptuse % 2]
                for i in range(4):
                    kt16 = g * 4 + i
                    ins = E.transpose(out=bank[:, i * 128:(i + 1) * 128],
                                      in_=self.vh[h % 2][:, kt16 * 128:(kt16 + 1) * 128],
                                      identity=c.consts.ident[:])
                E.set(ins, self.vtr[h][g])
                self._reg_ptev(ptuse, self.vev[h][g])
                ptuse += 1
            for qb in range(4):
                hqb = h * 4 + qb
                for qt in range(4):
                    if hqb * 4 + qt >= 1:
                        p = hqb * 4 + qt - 1
                        E.wait(self.expm[p // 4][p % 4])
                    for kb in range(4):
                        ins = E.matmul(out=self.sc[:, kb * 512:(kb + 1) * 512],
                                       lhsT=self.qh[h % 2][:, (qb * 4 + qt) * 128:(qb * 4 + qt + 1) * 128],
                                       rhs=self.kh[h % 2][:, kb * 512:(kb + 1) * 512],
                                       start=True, stop=True)
                    E.set(ins, self.scm[hqb][qt])
                    E.wait(self.pnm[hqb][qt])
                    for g in range(4):
                        if ptuse >= 2:
                            E.wait(self._ptev_flat(ptuse - 2))
                        bank = c.pt_banks[ptuse % 2]
                        for i in range(4):
                            kt16 = g * 4 + i
                            ins = E.transpose(out=bank[:, i * 128:(i + 1) * 128],
                                              in_=self.P[qt % 2][:, kt16 * 128:(kt16 + 1) * 128],
                                              identity=c.consts.ident[:])
                        E.set(ins, self.trg[hqb][qt][g])
                        self._reg_ptev(ptuse, self.ptev[hqb][qt][g])
                        ptuse += 1
                if hqb >= 1:
                    E.wait(self.oev[hqb - 1])
                for kt16 in range(16):
                    E.wait(self.ptev[hqb][3][kt16 // 4])
                    ins = E.matmul(out=c.p_banks[4][:],
                                   lhsT=self.vtok[:, kt16 * 128:(kt16 + 1) * 128],
                                   rhs=self.PT[:, kt16 * 512:(kt16 + 1) * 512],
                                   start=(kt16 == 0), stop=(kt16 == 15))
                E.set(ins, self.pvm[hqb])

    def _reg_ptev(self, idx, ms):
        if not hasattr(self, "_ptevs"):
            self._ptevs = {}
        self._ptevs[idx] = ms

    def _ptev_flat(self, idx):
        return self._ptevs[idx]

    def scalar(self, E):
        c = self.ctx
        ptuse = 0
        for h in range(HEADS):
            for g in range(4):
                E.wait(self.vtr[h][g])
                E.activation(out=self.vtok[:, g * 512:(g + 1) * 512],
                             in_=c.pt_banks[ptuse % 2][:, 0:512], func=AF.Copy)
                E.setd(self.vev[h][g])
                ptuse += 1
            for qb in range(4):
                hqb = h * 4 + qb
                for qt in range(4):
                    E.wait(self.mxm[hqb][qt])
                    for i in range(4):
                        ins = E.activation(out=self.P[qt % 2][:, i * 512:(i + 1) * 512],
                                           in_=self.sc[:, i * 512:(i + 1) * 512],
                                           func=AF.Exp, bias=self.stv[:, 1:2],
                                           scale=1.0 / math.sqrt(128.0),
                                           accum_out=self.sts[:, i:i + 1])
                    E.setd(self.expm[hqb][qt])
                    for g in range(4):
                        E.wait(self.trg[hqb][qt][g])
                        if hqb >= 1 and qt == 0 and g == 0:
                            E.wait(self.pvm[hqb - 1])
                        bank = c.pt_banks[ptuse % 2]
                        E.activation(out=self._pt_dst(g, qt),
                                      in_=bank[:, 0:512].rearrange("p (b c) -> p b c", b=4),
                                      func=AF.Copy)
                        E.setd(self.ptev[hqb][qt][g])
                        ptuse += 1
                E.wait(self.pvm[hqb])
                E.wait(self.ost[hqb - 2] if hqb >= 2 else None)
                E.activation(out=self.ostg[hqb % 2], in_=c.p_banks[4][:], func=AF.Copy)
                E.setd(self.oev[hqb])

    def _pt_dst(self, g, qt):
        full = self.PT.rearrange("p (g i q c) -> p g i q c", g=4, i=4, q=4)
        return full[:, g, :, qt, :]

    def vector(self, E):
        c = self.ctx
        for h in range(HEADS):
            for qb in range(4):
                hqb = h * 4 + qb
                for qt in range(4):
                    E.wait(self.scm[hqb][qt])
                    E.reduce_max(out=self.stv[:, 0:1], in_=self.sc[:], axis=AX.X, negate=True)
                    E.dchain()
                    E.tensor_scalar(out=self.stv[:, 1:2], in0=self.stv[:, 0:1],
                                    scalar1=1.0 / math.sqrt(128.0), scalar2=None, op0=ALU.mult)
                    E.setd(self.mxm[hqb][qt])
                    E.wait(self.expm[hqb][qt])
                    E.reduce_sum(out=self.stv[:, 2:3], in_=self.sts[:, 0:4], axis=AX.X)
                    E.dchain()
                    E.reciprocal(out=self.stv[:, 3:4], in_=self.stv[:, 2:3])
                    E.dchain()
                    E.tensor_scalar(out=self.P[qt % 2][:], in0=self.P[qt % 2][:],
                                    scalar1=self.stv[:, 3:4], scalar2=None, op0=ALU.mult)
                    E.setd(self.pnm[hqb][qt])

    def gpsimd(self, E):
        for h in range(HEADS):
            for qb in range(4):
                hqb = h * 4 + qb
                E.wait(self.oev[hqb])
                E.dma(self.o_d.ap()[h * 128:(h + 1) * 128, qb * 512:(qb + 1) * 512],
                      self.ostg[hqb % 2], ms=self.ost[hqb], slot=self.sl_o[hqb % 2])
        E.wait(self.ost[HEADS * 4 - 1])
        E.set(E.engine_nop(), self.done)


class P4_OPROJ:

    def __init__(self, ctx):
        self.ctx = ctx
        din = lambda n, s, t: ctx.dram(n, s, t, kind="in")
        self.o_d = ctx.dram("oT", [D, T], BF16)
        self.wo_d = din("wo_b", [32, KT, 128, 128], F32)
        self.g0_d = ctx.dram("g0T", [D, T], F32)
        self.xf_d = din("xfT", [D, T], F32)
        self.x1_d = ctx.dram("x1T", [D, T], F32)

    def alloc(self):
        ctx = self.ctx
        self.osb = ctx.BIG[:, 0:KT * T]
        self.wch = [ctx.sb(f"p4_wch{i}", [128, 128], F32) for i in range(3)]
        self.wbf = [ctx.sb(f"p4_wbf{i}", [128, 128], BF16) for i in range(3)]
        self.gch = [ctx.MIDF[:, i * 512:(i + 1) * 512] for i in range(2)]
        self.xch = [ctx.MIDF[:, 1024 + i * 512:1024 + (i + 1) * 512] for i in range(2)]
        self.tch = ctx.MIDF[:, 2048:2560]
        self.x1s = [ctx.MIDF[:, 2560 + i * 512:2560 + (i + 1) * 512] for i in range(2)]
        self.sl_w = [ctx.slot(f"p4w{i}") for i in range(3)]
        self.sl_o = ctx.slot("p4o")
        self.sl_g = [ctx.slot(f"p4g{i}") for i in range(2)]
        self.sl_x = [ctx.slot(f"p4x{i}") for i in range(2)]
        self.sl_st = [ctx.slot(f"p4st{i}") for i in range(2)]
        self.old = mss(KT, "p4old")
        self.wld = mss2(32, KT, "p4wld")
        self.wcv = mss2(32, KT, "p4wcv")
        self.tmm = mss(32 * KT, "p4tmm")
        self.mmog = [self.tmm[og * KT + KT - 1] for og in range(32)]
        self.gld = mss2(32, TB, "p4gld")
        self.xld = mss2(32, TB, "p4xld")
        self.cmb = mss2(32, TB, "p4cmb")
        self.x1st = mss2(32, TB, "p4x1st")
        self.done = Milestone("p4done")

    def prev_done(self):
        i = self.ctx.phases.index(self)
        return None if i == 0 else self.ctx.phases[i - 1].done

    def sync(self, E):
        E.wait(self.prev_done())
        for kt in range(KT):
            E.dma(self.osb[:, kt * T:(kt + 1) * T], self.o_d.ap()[kt * 128:(kt + 1) * 128, :],
                  ms=self.old[kt], slot=self.sl_o)
        idx = 0
        for og in range(32):
            for kt in range(KT):
                if idx >= 3:
                    p = idx - 3
                    E.wait(self.wcv[p // KT][p % KT])
                E.dma(self.wch[idx % 3][:], self.wo_d.ap()[og, kt],
                      ms=self.wld[og][kt], slot=self.sl_w[idx % 3])
                idx += 1
        E.wait(self.x1st[31][TB - 2])
        E.wait(self.x1st[31][TB - 1])
        E.set(E.nop(), self.done)

    def scalar(self, E):
        idx = 0
        for og in range(32):
            for kt in range(KT):
                E.wait(self.wld[og][kt])
                if idx >= 3:
                    E.wait(self.tmm[idx - 3])
                E.activation(out=self.wbf[idx % 3][:], in_=self.wch[idx % 3][:], func=AF.Copy)
                E.setd(self.wcv[og][kt])
                idx += 1

    def gpsimd(self, E):
        def store(i):
            og, tb = divmod(i, TB)
            E.wait(self.cmb[og][tb])
            E.dma(self.x1_d.ap()[og * 128:(og + 1) * 128, tb * 512:(tb + 1) * 512],
                  self.x1s[i % 2], ms=self.x1st[og][tb], slot=self.sl_st[i % 2])
        for og in range(32):
            for tb in range(TB):
                i = og * TB + tb
                if i >= 2:
                    E.wait(self.cmb[(i - 2) // TB][(i - 2) % TB])
                E.dma(self.gch[i % 2], self.g0_d.ap()[og * 128:(og + 1) * 128, tb * 512:(tb + 1) * 512],
                      ms=self.gld[og][tb], slot=self.sl_g[i % 2])
                E.dma(self.xch[i % 2], self.xf_d.ap()[og * 128:(og + 1) * 128, tb * 512:(tb + 1) * 512],
                      ms=self.xld[og][tb], slot=self.sl_x[i % 2])
                if i >= 1:
                    store(i - 1)
        store(32 * TB - 1)

    def vector(self, E):
        c = self.ctx
        for og in range(32):
            for tb in range(TB):
                i = og * TB + tb
                E.wait(self.mmog[og])
                E.wait(self.gld[og][tb])
                E.wait(self.xld[og][tb])
                if i >= 2:
                    E.wait(self.x1st[(i - 2) // TB][(i - 2) % TB])
                E.tensor_mul(out=self.tch, in0=self.gch[i % 2], in1=c.p_banks[tb][:])
                E.dchain()
                E.tensor_add(out=self.x1s[i % 2], in0=self.tch, in1=self.xch[i % 2])
                E.setd(self.cmb[og][tb])

    def tensor(self, E):
        c = self.ctx
        for kt in range(KT):
            E.wait(self.old[kt])
        idx = 0
        for og in range(32):
            for kt in range(KT):
                E.wait(self.wcv[og][kt])
                if kt == 0 and og >= 1:
                    E.wait(self.cmb[og - 1][TB - 1])
                for tb in range(TB):
                    ins = E.matmul(out=c.p_banks[tb][:],
                                   lhsT=self.wbf[idx % 3][:],
                                   rhs=self.osb[:, kt * T + tb * 512:kt * T + (tb + 1) * 512],
                                   start=(kt == 0), stop=(kt == KT - 1))
                E.set(ins, self.tmm[idx])
                idx += 1

class P6_FFN1:

    def __init__(self, ctx, dbg=False):
        self.ctx = ctx
        din = lambda n, s, t: ctx.dram(n, s, t, kind="in")
        self.wdn_d = din("wdn_b", [KT, 128, 128], F32)
        self.bdn_d = din("bdn", [128, 1], F32)
        self.e32_d = din("e32", [KT, 128, 32], F16)
        self.q_d = [din("gq_b", [86, 128, 16, 128], U8), din("uq_b", [86, 128, 16, 128], U8)]
        self.s_d = [din("gs_b", [86, 128, 16, 128], F16), din("us_b", [86, 128, 16, 128], F16)]
        self.z_d = [din("gz_b", [32, 5504], U8), din("uz_b", [32, 5504], U8)]
        self.sc_d = [din("gsc", [32, 11008], F16), din("usc", [32, 11008], F16)]
        self.hg_d = ctx.dram("hgeluT", [128, T], F16)
        self.int_d = ctx.dram("interT", [INTER, T], F16)
        self.dbg = dbg
        if dbg:
            self.wdbg_d = ctx.dram("p6_wdbg", [2, 128, 4096], BF16)
            self.gdbg_d = ctx.dram("p6_gdbg", [128, 2048], BF16)
            self.xdbg_d = ctx.dram("p6_xdbg", [32, 2048], BF16)

    def alloc(self):
        ctx = self.ctx
        A, F = ctx.MIDA, ctx.MIDF
        self.wbuf = [A[:, 0:4096].bitcast(F16), A[:, 4096:8192].bitcast(F16)]
        self.gate = A[:, 8192:10240].bitcast(F16)
        self.sils = A[:, 10240:10752].bitcast(F16)
        self.ints = [A[:, 10752 + i * 512:10752 + (i + 1) * 512].bitcast(F16) for i in range(2)]
        self.qsb = [A[:, 11776:13824].bitcast(U8)[:, i * 2048:(i + 1) * 2048] for i in range(2)]
        self.lo8 = A[:, 13824:14848].bitcast(U8)
        self.hi8 = A[:, 14848:15872].bitcast(U8)
        self.ssb = [F[:, 0:1024].bitcast(F16), F[:, 1024:2048].bitcast(F16)]
        self.stile = [F[0:32, 2048:2112].bitcast(F16), F[0:32, 2112:2176].bitcast(F16)]
        self.zsb = [F[0:32, 2176:3552].bitcast(U8), F[0:32, 3552:4928].bitcast(U8)]
        self.znib = F[0:32, 4928:4960].bitcast(U8)
        self.zpt = [F[0:32, 4960 + i * 64:4960 + (i + 1) * 64].bitcast(F16) for i in range(2)]
        self.xgn = F[0:32, 5088:6112].bitcast(F16)
        self.e32s = F[:, 6112:6624].bitcast(F16)
        self.hgs = F[:, 6624:7648].bitcast(F16)
        self.wch = [ctx.sb(f"p6_wch{i}", [128, 128], F32) for i in range(3)]
        self.wbf = [ctx.sb(f"p6_wbf{i}", [128, 128], F16) for i in range(3)]
        self.bdn = ctx.sb("p6_bdn", [128, 1], F32)
        self.sl_w = [ctx.slot(f"p6w{i}") for i in range(3)]
        self.sl_q = [ctx.slot(f"p6q{i}") for i in range(2)]
        self.sl_s = [ctx.slot(f"p6s{i}") for i in range(2)]
        self.sl_t = [ctx.slot(f"p6t{i}") for i in range(2)]
        self.sl_misc = ctx.slot("p6misc")
        self.sl_hg = ctx.slot("p6hg")
        self.sl_int = [ctx.slot(f"p6int{i}") for i in range(2)]
        self.miscld = Milestone("p6miscld")
        self.xgmm = mss(TB, "p6xgmm")
        self.xgev = mss(TB, "p6xgev")
        self.xgneg = Milestone("p6xgneg")
        self.wld = mss(KT, "p6wld")
        self.wcv = mss(KT, "p6wcv")
        self.dnmm = mss(KT, "p6dnmm")
        self.hgev = mss(TB, "p6hgev")
        self.hgst = mss(TB, "p6hgst")
        NS = 2 * 86
        self.qld = mss(NS, "p6qld")
        self.sld = mss(NS, "p6sld")
        self.tld = mss(NS, "p6tld")
        self.dq = mss(NS, "p6dq")
        self.zp = mss(NS, "p6zp")
        self.smm = mss2(NS, TB, "p6smm")
        self.gev = mss2(86, TB, "p6gev")
        self.sil = mss2(86, TB, "p6sil")
        self.imul = mss2(86, TB, "p6imul")
        self.ist = mss2(86, TB, "p6ist")
        self.done = Milestone("p6done")

    def prev_done(self):
        return self.ctx.phases[self.ctx.phases.index(self) - 1].done

    def sync(self, E):
        E.wait(self.prev_done())
        E.dma(self.bdn[:], self.bdn_d.ap(), slot=self.sl_misc)
        for kt in range(KT):
            E.dma(self.e32s[:, kt * 32:(kt + 1) * 32], self.e32_d.ap()[kt], slot=self.sl_misc)
        E.dma(self.zsb[0], self.z_d[0].ap(), slot=self.sl_misc)
        E.dma(self.zsb[1], self.z_d[1].ap(), ms=self.miscld, slot=self.sl_misc)
        idx = 0
        for kt in range(KT):
            if idx >= 3:
                E.wait(self.wcv[idx - 3])
            E.dma(self.wch[idx % 3][:], self.wdn_d.ap()[kt], ms=self.wld[kt], slot=self.sl_w[idx % 3])
            idx += 1
        for i in range(2 * 86):
            nt, m = divmod(i, 2)
            if i >= 2:
                E.wait(self.dq[i - 2])
            E.dma(self.qsb[i % 2], self.q_d[m].ap()[nt].rearrange("p k c -> p (k c)"),
                  ms=self.qld[i], slot=self.sl_q[i % 2])
            E.dma(self.ssb[i % 2], self.s_d[m].ap()[nt].rearrange("p k c -> p (k c)"),
                  ms=self.sld[i], slot=self.sl_s[i % 2])
            E.dma(self.stile[i % 2], self.sc_d[m].ap()[:, nt * 128:(nt + 1) * 128],
                  ms=self.tld[i], slot=self.sl_t[i % 2])

    def vector(self, E):
        c = self.ctx
        E.wait(self.miscld)
        for tb in range(TB):
            E.wait(self.xgev[tb])
        E.tensor_scalar(out=self.xgn, in0=self.xgn, scalar1=-1.0, scalar2=None, op0=ALU.mult)
        E.setd(self.xgneg)
        for i in range(2 * 86):
            nt, m = divmod(i, 2)
            E.wait(self.qld[i])
            E.wait(self.sld[i])
            if i >= 2:
                E.wait(self.smm[i - 2][TB - 1])
            wb = self.wbuf[i % 2]
            even = wb.rearrange("p (k w c) -> p k w c", w=2, c=128)[:, :, 0, :]
            odd = wb.rearrange("p (k w c) -> p k w c", w=2, c=128)[:, :, 1, :]
            qs = self.qsb[i % 2].rearrange("p (k c) -> p k c", c=128)
            ss = self.ssb[i % 2].rearrange("p (k c) -> p k c", c=128)
            E.tensor_single_scalar(out=self.lo8, in_=self.qsb[i % 2], scalar=15, op=ALU.bitwise_and)
            E.tensor_single_scalar(out=self.hi8, in_=self.qsb[i % 2], scalar=4, op=ALU.logical_shift_right)
            E.dchain()
            E.tensor_mul(out=even, in0=self.lo8.rearrange("p (k c) -> p k c", c=128), in1=ss)
            E.tensor_mul(out=odd, in0=self.hi8.rearrange("p (k c) -> p k c", c=128), in1=ss)
            E.setd(self.dq[i])
            E.wait(self.tld[i])
            b, par = divmod(nt, 2)
            op = ALU.bitwise_and if par == 0 else ALU.logical_shift_right
            sc = 15 if par == 0 else 4
            E.tensor_single_scalar(out=self.znib, in_=self.zsb[m][:, b * 128:(b + 1) * 128],
                                   scalar=sc, op=op)
            E.dchain()
            E.scalar_tensor_tensor(out=self.zpt[i % 2], in0=self.znib, scalar=1.0,
                                   in1=self.stile[i % 2], op0=ALU.add, op1=ALU.mult)
            E.setd(self.zp[i])
            if m == 1:
                for tb in range(TB):
                    E.wait(self.sil[nt][tb])
                    E.wait(self.smm[i][tb])
                    j = nt * TB + tb
                    if j >= 2:
                        E.wait(self.ist[(j - 2) // TB][(j - 2) % TB])
                    E.tensor_mul(out=self.ints[j % 2], in0=self.sils, in1=c.p_banks[tb][:])
                    E.setd(self.imul[nt][tb])

    def scalar(self, E):
        c = self.ctx
        for tb in range(TB):
            E.wait(self.xgmm[tb])
            E.activation(out=self.xgn[:, tb * 512:(tb + 1) * 512], in_=c.p_banks[4][0:32, 0:512],
                         func=AF.Copy)
            E.setd(self.xgev[tb])
        idx = 0
        for kt in range(KT):
            E.wait(self.wld[kt])
            if idx >= 3:
                E.wait(self.dnmm[idx - 3])
            E.activation(out=self.wbf[idx % 3][:], in_=self.wch[idx % 3][:], func=AF.Copy)
            E.setd(self.wcv[kt])
            idx += 1
        for tb in range(TB):
            E.wait(self.dnmm[KT - 1])
            E.activation(out=self.hgs[:, tb * 512:(tb + 1) * 512],
                         in_=c.p_banks[tb][:], func=AF.Gelu, bias=self.bdn[:, 0:1])
            E.setd(self.hgev[tb])
        for i in range(2 * 86):
            nt, m = divmod(i, 2)
            if m == 0:
                for tb in range(TB):
                    E.wait(self.smm[i][tb])
                    E.activation(out=self.gate[:, tb * 512:(tb + 1) * 512],
                                 in_=c.p_banks[tb][:], func=AF.Copy)
                    E.setd(self.gev[nt][tb])
            else:
                for tb in range(TB):
                    E.wait(self.gev[nt][tb])
                    if nt >= 1 or tb >= 1:
                        j = nt * TB + tb - 1
                        E.wait(self.imul[j // TB][j % TB])
                    E.activation(out=self.sils, in_=self.gate[:, tb * 512:(tb + 1) * 512],
                                 func=AF.Silu)
                    E.setd(self.sil[nt][tb])

    def gpsimd(self, E):
        for tb in range(TB):
            E.wait(self.hgev[tb])
            E.dma(self.hg_d.ap()[:, tb * 512:(tb + 1) * 512],
                  self.hgs[:, tb * 512:(tb + 1) * 512],
                  ms=self.hgst[tb], slot=self.sl_hg)
        for nt in range(86):
            for tb in range(TB):
                j = nt * TB + tb
                E.wait(self.imul[nt][tb])
                E.dma(self.int_d.ap()[nt * 128:(nt + 1) * 128, tb * 512:(tb + 1) * 512],
                      self.ints[j % 2], ms=self.ist[nt][tb], slot=self.sl_int[j % 2])
        if getattr(self, 'dbg', False):
            E.wait(self.dq[171])
            E.dma(self.wdbg_d.ap()[0], self.wbuf[0])
            E.dma(self.wdbg_d.ap()[1], self.wbuf[1])
            E.wait(self.gev[85][TB - 1])
            E.dma(self.gdbg_d.ap(), self.gate)
            E.wait(self.xgneg)
            E.dma(self.xdbg_d.ap(), self.xgn)
        E.wait(self.ist[85][TB - 1])
        E.set(E.nop(), self.done)

    def tensor(self, E):
        c = self.ctx
        ln1 = None
        for ph in self.ctx.phases:
            if isinstance(ph, P_LN) and ph.pref == "ln1":
                ln1 = ph
        xn = ln1.xn
        E.wait(self.miscld)
        for tb in range(TB):
            if tb >= 1:
                E.wait(self.xgev[tb - 1])
            for kt in range(KT):
                ins = E.matmul(out=c.p_banks[4][0:32, 0:512],
                               lhsT=self.e32s[:, kt * 32:(kt + 1) * 32],
                               rhs=xn[:, kt * T + tb * 512:kt * T + (tb + 1) * 512],
                               start=(kt == 0), stop=(kt == KT - 1))
            E.set(ins, self.xgmm[tb])
        idx = 0
        for kt in range(KT):
            E.wait(self.wcv[kt])
            for tb in range(TB):
                ins = E.matmul(out=c.p_banks[tb][:],
                               lhsT=self.wbf[idx % 3][:],
                               rhs=xn[:, kt * T + tb * 512:kt * T + (tb + 1) * 512],
                               start=(kt == 0), stop=(kt == KT - 1))
            E.set(ins, self.dnmm[kt])
            idx += 1
        for i in range(2 * 86):
            nt, m = divmod(i, 2)
            E.wait(self.dq[i])
            E.wait(self.zp[i])
            E.wait(self.xgneg)
            for tb in range(TB):
                if i == 0:
                    E.wait(self.hgev[tb])
                elif i % 2 == 0:
                    E.wait(self.imul[(i - 2) // 2][tb])
                else:
                    E.wait(self.gev[i // 2][tb])
                for kt in range(KT):
                    E.matmul(out=c.p_banks[tb][:],
                             lhsT=self.wbuf[i % 2][:, kt * 128:(kt + 1) * 128],
                             rhs=xn[:, kt * T + tb * 512:kt * T + (tb + 1) * 512],
                             start=(kt == 0), stop=False)
                ins = E.matmul(out=c.p_banks[tb][:],
                               lhsT=self.zpt[i % 2],
                               rhs=self.xgn[:, tb * 512:(tb + 1) * 512],
                               start=False, stop=True)
                E.set(ins, self.smm[i][tb])

class P7_HYDRA:

    PAIRS = [(t, tp) for tp in range(8) for t in range(tp, 8)]

    def __init__(self, ctx):
        self.ctx = ctx
        din = lambda n, s, t: ctx.dram(n, s, t, kind="in")
        self.u_d = ctx.dram("hyuT", [128, T], F16)
        self.inwT_d = din("hy_inwT", [128, 576], F32)
        self.cw_d = din("hy_cwrep", [128, 7 * 288], F16)
        self.cb_d = din("hy_cbrep", [128, 288], F32)
        self.dtb_d = din("hy_dtbias", [128, 64], F32)
        self.alog_d = din("hy_alog", [128, 4], F32)
        self.drep_d = din("hy_drep", [128, 32], F32)
        self.fcd_d = din("hy_fcdrep", [128, 1024], F16)
        self.nw_d = din("hy_nwrep", [128, 256], F16)
        self.outwT_d = din("hy_outwT", [256, 128], F32)
        self.g1_d = din("hy_g1", [128, 1], F32)
        self.h4_d = ctx.dram("h4T", [128, T], F16)

    def alloc(self):
        ctx = self.ctx
        A, F = ctx.MIDA, ctx.MIDF
        cf = lambda ap: ap.bitcast(F16)
        self.ufm = cf(A[:, 0:2048])
        self.u2 = cf(A[:, 2048:4096])
        self.ynfm = [cf(A[:, 4096 + i * 2048:4096 + (i + 1) * 2048]) for i in range(2)]
        self.h4 = cf(A[:, 8192:10240])
        self.ctmp = [cf(A[:, 8192 + i * 288:8192 + (i + 1) * 288]) for i in range(2)]
        self.nw = cf(A[:, 8768:9024])
        self.inwb = cf(A[:, 10240:10816])
        self.outwb = cf(A[:, 10816:11072])
        self.cw = cf(A[:, 11072:13088])
        self.fcd = cf(A[:, 13088:14112])
        self.u3 = cf(A[:, 14112:16160])
        self.zxt = F[:, 0:4608]
        self.xbs = F[:, 4608:6912]
        self.dtraw = F[:, 6912:6976]
        self.dtfin = F[:, 6976:7040]
        self.sA = F[:, 7040:7104]
        self.pd = F[:, 7104:7248]
        self.pw = F[:, 7248:7392]
        self.gall = F[:, 7392:7428]
        self.fct = [F[:, 7428 + i * 256:7428 + (i + 1) * 256] for i in range(2)]
        self.fcb = F[:, 7940:7972]
        self.st8 = F[:, 7972:7988]
        self.smf = F[:, 7988:8000]
        self.gtmp = [self.zxt[:, i * 512:(i + 1) * 512] for i in range(2)]
        self.cbf = ctx.sb("p7_cbf", [128, 288], F32)
        self.inwf = ctx.sb("p7_inwf", [128, 576], F32)
        self.outwf = ctx.sb("p7_outwf", [128, 256], F32)
        self.dbia = ctx.sb("p7_dbia", [128, 64], F32)
        self.alog = ctx.sb("p7_alog", [128, 4], F32)
        self.drp = ctx.sb("p7_drp", [128, 32], F32)
        self.g1c = ctx.sb("p7_g1c", [128, 1], F32)
        self.arep = ctx.sb("p7_arep", [128, 4], F32)
        self.sl_p = ctx.slot("p7p")
        self.sl_h4 = ctx.slot("p7h4")
        self.pld = Milestone("p7pld")
        self.pcv = Milestone("p7pcv")
        self.arpm = Milestone("p7arpm")
        self.gsig = Milestone("p7gsig")

        def S(tag):
            return [[Milestone(f"p7{tag}{a}{s}") for s in range(2)] for a in range(2)]
        self.inmm = [[[mss(2, "p7inmm") for _ in range(8)] for _ in range(2)] for _ in range(2)]
        self.inev = [[mss(8, "p7inev") for _ in range(2)] for _ in range(2)]
        self.zsm = S("zsm")
        self.convv = S("convv")
        self.silm = S("silm")
        self.dt_v1 = S("dtv1")
        self.dt_s1 = S("dts1")
        self.pex = [[mss(2, "p7pex") for _ in range(2)] for _ in range(2)]
        self.pwm = [[mss(2, "p7pwm") for _ in range(2)] for _ in range(2)]
        self.rmsv = S("rmsv")
        self.rmss = S("rmss")
        self.ynm = S("ynm")
        self.trm = [[mss(16, "p7trm") for _ in range(2)] for _ in range(2)]
        self.tev = [[mss(16, "p7tev") for _ in range(2)] for _ in range(2)]
        self.omm = [mss(4, "p7omm") for _ in range(2)]
        self.oev = [mss(4, "p7oev") for _ in range(2)]
        self.gat = mss(4, "p7gat")
        self.h4st = mss(4, "p7h4st")
        self.done = Milestone("p7done")

    def prev_done(self):
        i = self.ctx.phases.index(self)
        return None if i == 0 else self.ctx.phases[i - 1].done

    def _yt(self, t, h=None):
        base = t * 576 + 256
        if h is None:
            return self.zxt[:, base:base + 256]
        return self.zxt[:, base + h * 64:base + (h + 1) * 64]

    def _xs(self, l, h=None):
        if h is None:
            return self.xbs[:, l * 288:l * 288 + 256]
        return self.xbs[:, l * 288 + h * 64:l * 288 + (h + 1) * 64]

    def _z(self, l):
        return self.zxt[:, l * 576:l * 576 + 256]

    def sync(self, E):
        E.wait(self.prev_done())
        E.dma(self.inwf[:], self.inwT_d.ap(), slot=self.sl_p)
        E.dma(self.cw, self.cw_d.ap(), slot=self.sl_p)
        E.dma(self.cbf[:], self.cb_d.ap(), slot=self.sl_p)
        E.dma(self.dbia[:], self.dtb_d.ap(), slot=self.sl_p)
        E.dma(self.alog[:], self.alog_d.ap(), slot=self.sl_p)
        E.dma(self.drp[:], self.drep_d.ap(), slot=self.sl_p)
        E.dma(self.fcd, self.fcd_d.ap(), slot=self.sl_p)
        E.dma(self.nw, self.nw_d.ap(), slot=self.sl_p)
        E.dma(self.outwf[:, 0:128], self.outwT_d.ap()[0:128, :], slot=self.sl_p)
        E.dma(self.outwf[:, 128:256], self.outwT_d.ap()[128:256, :], slot=self.sl_p)
        E.dma(self.g1c[:], self.g1_d.ap(), slot=self.sl_p)
        E.dma(self.ufm, self.u_d.ap(), ms=self.pld, slot=self.sl_p)
        for ch in range(4):
            E.wait(self.gat[ch])
            E.dma(self.h4_d.ap()[:, ch * 512:(ch + 1) * 512],
                  self.h4[:, ch * 512:(ch + 1) * 512], ms=self.h4st[ch], slot=self.sl_h4)
        E.wait(self.h4st[3])
        E.set(E.nop(), self.done)

    def scalar(self, E):
        c = self.ctx
        E.wait(self.pld)
        E.activation(out=self.inwb, in_=self.inwf[:], func=AF.Copy)
        E.activation(out=self.outwb[:, 0:128], in_=self.outwf[:, 0:128], func=AF.Copy)
        E.activation(out=self.outwb[:, 128:256], in_=self.outwf[:, 128:256], func=AF.Copy)
        E.setd(self.pcv)
        E.activation(out=self.arep[:], in_=self.alog[:], func=AF.Exp)
        E.setd(self.arpm)
        E.activation(out=self.smf[:, 4:5], in_=self.g1c[:], func=AF.Sigmoid)
        E.setd(self.gsig)
        for a in range(2):
            for s in range(2):
                for l in range(8):
                    E.wait(self.inmm[a][s][l][1])
                    pb = (l % 2) * 1024
                    E.activation(out=self.zxt[:, l * 576:l * 576 + 512],
                                 in_=c.sc4[:, pb:pb + 512], func=AF.Copy)
                    E.activation(out=self.zxt[:, l * 576 + 512:l * 576 + 576],
                                 in_=c.sc4[:, pb + 512:pb + 576], func=AF.Copy)
                    E.setd(self.inev[a][s][l])
                zs = self.zxt.rearrange("p (l c) -> p l c", c=576)[:, :, 0:256]
                E.activation(out=zs, in_=zs, func=AF.Silu)
                E.setd(self.zsm[a][s])
                E.wait(self.convv[a][s])
                E.activation(out=self.xbs, in_=self.xbs, func=AF.Silu)
                E.setd(self.silm[a][s])
                E.wait(self.dt_v1[a][s])
                E.activation(out=self.dtraw, in_=self.dtraw, func=AF.Abs)
                E.dchain()
                E.activation(out=self.dtraw, in_=self.dtraw, func=AF.Exp, scale=-1.0)
                E.dchain()
                E.activation(out=self.dtraw, in_=self.dtraw, func=AF.Ln,
                             bias=c.consts.ones_f[:, 0:1])
                E.setd(self.dt_s1[a][s])
                for d in range(2):
                    E.wait(self.pex[a][s][d])
                    E.activation(out=self.pw, in_=self.pd, func=AF.Exp)
                    E.setd(self.pwm[a][s][d])
                E.wait(self.rmsv[a][s])
                E.activation(out=self.st8[:, 8:16], in_=self.st8[:, 0:8], func=AF.Ln)
                E.dchain()
                E.activation(out=self.st8[:, 8:16], in_=self.st8[:, 8:16], func=AF.Exp, scale=-0.5)
                E.setd(self.rmss[a][s])
                for i in range(16):
                    E.wait(self.trm[a][s][i])
                    t, ci = divmod(i, 2)
                    bank = c.p_banks[4 + (i % 2)]
                    E.activation(out=self.ynfm[ci][:, t * 256 + s * 128:t * 256 + s * 128 + 128],
                                 in_=bank[:, 0:128], func=AF.Copy)
                    E.setd(self.tev[a][s][i])
            dst = self.u2 if a == 0 else self.u3
            for ch in range(4):
                E.wait(self.omm[a][ch])
                pb = (ch % 2) * 1024
                E.activation(out=dst[:, ch * 512:(ch + 1) * 512],
                             in_=c.sc4[:, pb:pb + 512], func=AF.Copy)
                E.setd(self.oev[a][ch])

    def vector(self, E):
        E.wait(self.arpm)
        E.tensor_scalar(out=self.arep[:], in0=self.arep[:], scalar1=-1.0, scalar2=None,
                        op0=ALU.mult)
        E.wait(self.gsig)
        E.tensor_scalar(out=self.smf[:, 5:6], in0=self.smf[:, 4:5], scalar1=1.0, scalar2=None,
                        op0=ALU.add)
        E.tensor_scalar(out=self.smf[:, 6:7], in0=self.smf[:, 4:5], scalar1=-1.0, scalar2=1.0,
                        op0=ALU.mult, op1=ALU.add)
        E.dchain()
        for a in range(2):
            for s in range(2):
                E.wait(self.inev[a][s][7])
                co = lambda l: self.xbs[:, l * 288:(l + 1) * 288]
                xi = lambda l: self.zxt[:, l * 576 + 256:l * 576 + 544]
                for l in range(8):
                    first = True
                    for j in range(7):
                        lsrc = l - 6 + j
                        if lsrc < 0:
                            continue
                        w = self.cw[:, j * 288:(j + 1) * 288]
                        if first:
                            E.tensor_mul(out=co(l), in0=xi(lsrc), in1=w)
                            first = False
                        else:
                            E.tensor_mul(out=self.ctmp[j % 2], in0=xi(lsrc), in1=w)
                            E.dchain()
                            E.tensor_add(out=co(l), in0=co(l), in1=self.ctmp[j % 2])
                            E.dchain()
                    E.dchain()
                    E.tensor_add(out=co(l), in0=co(l), in1=self.cbf[:])
                    E.dchain()
                E.setd(self.convv[a][s])
                for l in range(8):
                    E.tensor_copy(out=self.dtraw[:, l * 4:(l + 1) * 4],
                                  in_=self.zxt[:, l * 576 + 544:l * 576 + 548])
                    E.tensor_copy(out=self.dtraw[:, 32 + (7 - l) * 4:32 + (8 - l) * 4],
                                  in_=self.zxt[:, l * 576 + 548:l * 576 + 552])
                E.dchain()
                E.tensor_add(out=self.dtraw, in0=self.dtraw, in1=self.dbia[:])
                E.dchain()
                E.tensor_copy(out=self.dtfin, in_=self.dtraw)
                E.setd(self.dt_v1[a][s])
                E.wait(self.dt_s1[a][s])
                E.tensor_scalar(out=self.dtfin, in0=self.dtfin, scalar1=0.0, scalar2=None,
                                op0=ALU.max)
                E.dchain()
                E.tensor_add(out=self.dtfin, in0=self.dtfin, in1=self.dtraw)
                E.dchain()
                for d in range(2):
                    off = d * 32
                    for l in range(8):
                        E.tensor_mul(out=self.sA[:, off + l * 4:off + (l + 1) * 4],
                                     in0=self.dtfin[:, off + l * 4:off + (l + 1) * 4],
                                     in1=self.arep[:])
                E.dchain()
                for d in range(2):
                    off = d * 32
                    for l in range(1, 8):
                        E.tensor_add(out=self.sA[:, off + l * 4:off + (l + 1) * 4],
                                     in0=self.sA[:, off + l * 4:off + (l + 1) * 4],
                                     in1=self.sA[:, off + (l - 1) * 4:off + l * 4])
                        E.dchain()
                for t in range(8):
                    E.tensor_scalar(out=self._yt(t), in0=self._yt(t), scalar1=0.0,
                                    scalar2=None, op0=ALU.mult)
                E.dchain()
                E.wait(self.silm[a][s])
                for d in range(2):
                    off = d * 32
                    for pi, (t, tp) in enumerate(self.PAIRS):
                        E.tensor_sub(out=self.pd[:, pi * 4:(pi + 1) * 4],
                                     in0=self.sA[:, off + t * 4:off + (t + 1) * 4],
                                     in1=self.sA[:, off + tp * 4:off + (tp + 1) * 4])
                    for pg in range(0, 36, 2):
                        for k in range(2):
                            t, tp = self.PAIRS[pg + k]
                            ct, ctp = (t, tp) if d == 0 else (7 - t, 7 - tp)
                            E.tensor_mul(out=self.fct[k][:, 0:16],
                                         in0=self.xbs[:, ctp * 288 + 256:ctp * 288 + 272],
                                         in1=self.xbs[:, ct * 288 + 272:ct * 288 + 288])
                        E.dchain()
                        for k in range(2):
                            E.reduce_sum(out=self.gall[:, pg + k:pg + k + 1],
                                         in_=self.fct[k][:, 0:16], axis=AX.X)
                        E.dchain()
                    E.setd(self.pex[a][s][d])
                    E.wait(self.pwm[a][s][d])
                    for pi, (t, tp) in enumerate(self.PAIRS):
                        E.tensor_mul(out=self.pw[:, pi * 4:(pi + 1) * 4],
                                     in0=self.pw[:, pi * 4:(pi + 1) * 4],
                                     in1=self.dtfin[:, off + tp * 4:off + (tp + 1) * 4])
                    E.dchain()
                    for pi in range(36):
                        E.tensor_scalar(out=self.pw[:, pi * 4:(pi + 1) * 4],
                                        in0=self.pw[:, pi * 4:(pi + 1) * 4],
                                        scalar1=self.gall[:, pi:pi + 1], scalar2=None,
                                        op0=ALU.mult)
                    E.dchain()
                    for t in range(8):
                        dst_t = t + 1 if d == 0 else 6 - t
                        if dst_t < 0 or dst_t > 7:
                            continue
                        for tp in range(t + 1):
                            pi = self.PAIRS.index((t, tp))
                            src = tp if d == 0 else 7 - tp
                            for h in range(4):
                                E.scalar_tensor_tensor(
                                    out=self._yt(dst_t, h), in0=self._xs(src, h),
                                    scalar=self.pw[:, pi * 4 + h:pi * 4 + h + 1],
                                    in1=self._yt(dst_t, h), op0=ALU.mult, op1=ALU.add)
                    E.dchain()
                for t in range(8):
                    for h in range(4):
                        i = t * 4 + h
                        E.tensor_mul(out=self.fct[0], in0=self._xs(t),
                                     in1=self.fcd[:, h * 256:(h + 1) * 256])
                        E.dchain()
                        E.reduce_sum(out=self.fcb[:, i:i + 1], in_=self.fct[0], axis=AX.X)
                        E.dchain()
                E.tensor_add(out=self.fcb, in0=self.fcb, in1=self.drp[:])
                E.dchain()
                for t in range(8):
                    for h in range(4):
                        E.scalar_tensor_tensor(
                            out=self._yt(t, h), in0=self._xs(t, h),
                            scalar=self.fcb[:, t * 4 + h:t * 4 + h + 1],
                            in1=self._yt(t, h), op0=ALU.mult, op1=ALU.add)
                E.dchain()
                E.wait(self.zsm[a][s])
                for t in range(8):
                    E.tensor_mul(out=self._yt(t), in0=self._yt(t), in1=self._z(t))
                E.dchain()
                for t in range(8):
                    E.tensor_mul(out=self.fct[0], in0=self._yt(t), in1=self._yt(t))
                    E.dchain()
                    E.reduce_sum(out=self.st8[:, t:t + 1], in_=self.fct[0], axis=AX.X)
                    E.dchain()
                E.tensor_scalar(out=self.st8[:, 0:8], in0=self.st8[:, 0:8],
                                scalar1=1.0 / 256.0, scalar2=1e-5, op0=ALU.mult, op1=ALU.add)
                E.setd(self.rmsv[a][s])
                E.wait(self.rmss[a][s])
                for t in range(8):
                    E.tensor_scalar(out=self._yt(t), in0=self._yt(t),
                                    scalar1=self.st8[:, 8 + t:9 + t], scalar2=None,
                                    op0=ALU.mult)
                E.dchain()
                for t in range(8):
                    E.tensor_mul(out=self._yt(t), in0=self._yt(t), in1=self.nw)
                E.setd(self.ynm[a][s])
        for ch in range(4):
            E.wait(self.oev[1][ch])
            E.tensor_scalar(out=self.gtmp[ch % 2], in0=self.ufm[:, ch * 512:(ch + 1) * 512],
                            scalar1=self.smf[:, 5:6], scalar2=None, op0=ALU.mult)
            E.dchain()
            E.scalar_tensor_tensor(out=self.h4[:, ch * 512:(ch + 1) * 512],
                                   in0=self.u3[:, ch * 512:(ch + 1) * 512],
                                   scalar=self.smf[:, 6:7],
                                   in1=self.gtmp[ch % 2], op0=ALU.mult, op1=ALU.add)
            E.setd(self.gat[ch])

    def gpsimd(self, E):
        pass

    def tensor(self, E):
        c = self.ctx
        E.wait(self.prev_done())
        E.wait(self.pld)
        E.wait(self.pcv)
        for a in range(2):
            src = self.ufm if a == 0 else self.u2
            for s in range(2):
                for l in range(8):
                    li = (a * 2 + s) * 8 + l
                    if li >= 2:
                        pa, r = divmod(li - 2, 16)
                        ps_, pl = divmod(r, 8)
                        E.wait(self.inev[pa][ps_][pl])
                    if a == 1 and s == 0 and l == 0:
                        E.wait(self.oev[0][3])
                    pb = (l % 2) * 1024
                    lhsT = src[:, (l * 256 + s * 128):(l * 256 + s * 128 + 128)]
                    ins = E.matmul(out=c.sc4[:, pb:pb + 512], lhsT=lhsT,
                                   rhs=self.inwb[:, 0:512], start=True, stop=True)
                    E.set(ins, self.inmm[a][s][l][0])
                    ins = E.matmul(out=c.sc4[:, pb + 512:pb + 576], lhsT=lhsT,
                                   rhs=self.inwb[:, 512:576], start=True, stop=True)
                    E.set(ins, self.inmm[a][s][l][1])
                E.wait(self.ynm[a][s])
                for i in range(16):
                    t, ci = divmod(i, 2)
                    gi = (a * 2 + s) * 16 + i
                    if i >= 2:
                        E.wait(self.tev[a][s][i - 2])
                    elif gi >= 2:
                        pa, r = divmod(gi - 2, 32)
                        ps_, pii = divmod(r, 16)
                        E.wait(self.tev[pa][ps_][pii])
                    bank = c.p_banks[4 + (i % 2)]
                    ins = E.transpose(out=bank[:, 0:128],
                                      in_=self.zxt[:, t * 576 + 256 + ci * 128:
                                                   t * 576 + 256 + (ci + 1) * 128],
                                      identity=c.consts.ident_f[:])
                    E.set(ins, self.trm[a][s][i])
            for ch in range(4):
                if ch >= 2:
                    E.wait(self.oev[a][ch - 2])
                pb = (ch % 2) * 1024
                E.matmul(out=c.sc4[:, pb:pb + 512], lhsT=self.outwb[:, 0:128],
                         rhs=self.ynfm[0][:, ch * 512:(ch + 1) * 512], start=True, stop=False)
                ins = E.matmul(out=c.sc4[:, pb:pb + 512], lhsT=self.outwb[:, 128:256],
                               rhs=self.ynfm[1][:, ch * 512:(ch + 1) * 512], start=False,
                               stop=True)
                E.set(ins, self.omm[a][ch])

    @staticmethod
    def host_params(inw, convw, convb, dtbias, alog, Dh, fcdw, normw, outw, gate1):
        inwT = np.zeros((128, 576), np.float32)
        inwT[:, 0:552] = np.asarray(inw, np.float32).T
        cw = np.zeros((128, 7 * 288), np.float16)
        wj = np.asarray(convw, np.float32)[:, 0, :]
        for j in range(7):
            cw[:, j * 288:(j + 1) * 288] = np.tile(wj[:, j], (128, 1)).astype(np.float16)
        return {
            "hy_inwT": inwT,
            "hy_cwrep": cw,
            "hy_cbrep": np.tile(np.asarray(convb, np.float32), (128, 1)),
            "hy_dtbias": np.tile(np.asarray(dtbias, np.float32), (128, 16)),
            "hy_alog": np.tile(np.asarray(alog, np.float32), (128, 1)),
            "hy_drep": np.tile(np.asarray(Dh, np.float32), (128, 8)),
            "hy_fcdrep": np.tile(np.asarray(fcdw, np.float32).reshape(1, 1024),
                                 (128, 1)).astype(np.float16),
            "hy_nwrep": np.tile(np.asarray(normw, np.float32), (128, 1)).astype(np.float16),
            "hy_outwT": np.ascontiguousarray(np.asarray(outw, np.float32).T),
            "hy_g1": np.full((128, 1), np.float32(gate1)),
        }


class P8_UP:

    def __init__(self, ctx):
        self.ctx = ctx
        din = lambda n, s, t: ctx.dram(n, s, t, kind="in")
        self.h5_d = ctx.dram("h5T", [128, T], F16)
        self.xn1_d = ctx.dram("xn1T", [D, T], F16)
        self.wup_d = din("wup_b", [KT, 128, 128], F16)
        self.bup_d = din("bupT", [128, KT], F32)
        self.s_d = ctx.dram("sT", [D, T], F32)

    def alloc(self):
        ctx = self.ctx
        A, F = ctx.MIDA, ctx.MIDF
        self.h5 = A[:, 0:2048].bitcast(F16)
        self.wup = A[:, 2048:6144].bitcast(F16)
        self.xch = [A[:, 6144 + i * 512:6144 + (i + 1) * 512].bitcast(F16) for i in range(2)]
        self.stg = [F[:, i * 512:(i + 1) * 512] for i in range(2)]
        self.bup = ctx.sb("p8_bup", [128, KT], F32)
        self.sl_p = ctx.slot("p8p")
        self.sl_x = [ctx.slot(f"p8x{i}") for i in range(2)]
        self.sl_st = [ctx.slot(f"p8st{i}") for i in range(2)]
        self.pld = Milestone("p8pld")
        self.mm = mss(KT, "p8mm")
        self.xld = mss(KT * TB, "p8xld")
        self.cmb = mss(KT * TB, "p8cmb")
        self.st = mss(KT * TB, "p8st")
        self.done = Milestone("p8done")

    def prev_done(self):
        i = self.ctx.phases.index(self)
        return None if i == 0 else self.ctx.phases[i - 1].done

    def sync(self, E):
        E.wait(self.prev_done())
        E.dma(self.bup[:], self.bup_d.ap(), slot=self.sl_p)
        E.dma(self.h5, self.h5_d.ap(), slot=self.sl_p)
        for kt in range(KT):
            E.dma(self.wup[:, kt * 128:(kt + 1) * 128], self.wup_d.ap()[kt],
                  ms=self.pld if kt == KT - 1 else None, slot=self.sl_p)
        for kt in range(KT):
            for tb in range(TB):
                i = kt * TB + tb
                E.wait(self.cmb[i])
                E.dma(self.s_d.ap()[kt * 128:(kt + 1) * 128, tb * 512:(tb + 1) * 512],
                      self.stg[i % 2], ms=self.st[i], slot=self.sl_st[i % 2])
        E.wait(self.st[KT * TB - 1])
        E.set(E.nop(), self.done)

    def gpsimd(self, E):
        E.wait(self.pld)
        for kt in range(KT):
            for tb in range(TB):
                i = kt * TB + tb
                if i >= 2:
                    E.wait(self.cmb[i - 2])
                E.dma(self.xch[i % 2],
                      self.xn1_d.ap()[kt * 128:(kt + 1) * 128, tb * 512:(tb + 1) * 512],
                      ms=self.xld[i], slot=self.sl_x[i % 2])

    def scalar(self, E):
        pass

    def vector(self, E):
        c = self.ctx
        for kt in range(KT):
            for tb in range(TB):
                i = kt * TB + tb
                E.wait(self.mm[kt])
                E.wait(self.xld[i])
                if i >= 2:
                    E.wait(self.st[i - 2])
                E.tensor_scalar(out=self.stg[i % 2], in0=c.p_banks[tb][:],
                                scalar1=self.bup[:, kt:kt + 1], scalar2=None, op0=ALU.add)
                E.dchain()
                E.tensor_add(out=self.stg[i % 2], in0=self.stg[i % 2], in1=self.xch[i % 2])
                E.setd(self.cmb[i])

    def tensor(self, E):
        c = self.ctx
        E.wait(self.prev_done())
        E.wait(self.pld)
        for kt in range(KT):
            for tb in range(TB):
                if kt >= 1:
                    E.wait(self.cmb[(kt - 1) * TB + tb])
                ins = E.matmul(out=c.p_banks[tb][:],
                               lhsT=self.wup[:, kt * 128:(kt + 1) * 128],
                               rhs=self.h5[:, tb * 512:(tb + 1) * 512],
                               start=True, stop=True)
            E.set(ins, self.mm[kt])

    @staticmethod
    def host_inputs(w_up, b_up):
        wb = np.asarray(w_up, np.float32)[PH].reshape(KT, 128, 128)
        wup_b = np.ascontiguousarray(wb.transpose(0, 2, 1)).astype(np.float16)
        bup = np.asarray(b_up, np.float32)[PH].reshape(KT, 128).T
        return {"wup_b": wup_b, "bupT": np.ascontiguousarray(bup)}


P9SEC = [(0, 11), (11, 22), (22, 33), (33, 43)]


class P9_FFN2:

    NI = KT * TB

    def __init__(self, ctx, dbg=False, nozp=False):
        self.ctx = ctx
        self.dbg = dbg
        self.nozp = nozp
        din = lambda n, s, t: ctx.dram(n, s, t, kind="in")
        self.int_d = ctx.dram("interT", [INTER, T], F16)
        self.s_d = ctx.dram("sT", [D, T], F32)
        self.x1_d = ctx.dram("x1T", [D, T], F32)
        if dbg:
            self.wdbg_d = ctx.dram("p9_wdbg", [128, 11008], F16)
            self.xdbg_d = ctx.dram("p9_xdbg", [128, 512], F16)
            self.zdbg_d = ctx.dram("p9_zdbg", [128, 128], F16)
            self.fdbg_d = ctx.dram("p9_fdbg", [128, 512], F16)
        self.dq_d = din("dq_b", [KT, 128, 43, 128], U8)
        self.ds_d = din("ds_b", [KT, 128, 43, 128], F16)
        self.dz_d = din("dz_b", [128, 2048], U8)
        self.dsc_d = din("dscP", [128, 4096], F16)
        self.e86_d = din("e86", [IKT, 128, 128], F16)
        self.out_d = ctx.dram("outT", [D, T], F32)

    def alloc(self):
        ctx = self.ctx
        A, F, B = ctx.MIDA, ctx.MIDF, ctx.BIG
        self.inter = B[:, 0:44032].bitcast(F16)
        self.e86 = B[:, 44032:55040].bitcast(F16)
        self.qsb = [B[:, 55040 + i * 704:55040 + (i + 1) * 704].bitcast(U8) for i in range(2)]
        self.lo8 = B[:, 56448:57152].bitcast(U8)
        self.hi8 = B[:, 57152:57856].bitcast(U8)
        self.ssb = [B[:, 57856 + i * 1408:57856 + (i + 1) * 1408].bitcast(F16) for i in range(2)]
        self.wbuf = A[:, 0:11008].bitcast(F16)
        self.sch = [F[:, i * 512:(i + 1) * 512] for i in range(2)]
        self.xch = [F[:, 1024 + i * 512:1024 + (i + 1) * 512] for i in range(2)]
        self.ost = [F[:, 2048 + i * 512:2048 + (i + 1) * 512] for i in range(2)]
        self.g1s = [F[:, 3072 + i * 512:3072 + (i + 1) * 512] for i in range(2)]
        self.f16g = [F[:, 4096 + i * 256:4096 + (i + 1) * 256].bitcast(F16) for i in range(2)]
        self.dsc = F[:, 4608:6656].bitcast(F16)
        self.dzb = F[:, 6656:7168].bitcast(U8)
        self.xgn = F[:, 7168:7424].bitcast(F16)
        self.zpt = [F[:, 7424 + i * 64:7424 + (i + 1) * 64].bitcast(F16) for i in range(2)]
        self.znib = F[:, 7552:7584].bitcast(U8)
        self.sl_int = ctx.slot("p9int")
        self.sl_misc = ctx.slot("p9misc")
        self.sl_q = [ctx.slot(f"p9q{i}") for i in range(2)]
        self.sl_s = [ctx.slot(f"p9s{i}") for i in range(2)]
        self.sl_sx = [ctx.slot(f"p9sx{i}") for i in range(2)]
        self.sl_st = [ctx.slot(f"p9st{i}") for i in range(2)]
        NI = TB * KT
        self.miscld = Milestone("p9miscld")
        self.ild = mss2(TB, IKT, "p9ild")
        self.xgmm = mss(TB, "p9xgmm")
        self.xgm = mss(TB, "p9xgm")
        self.xneg = mss(TB, "p9xneg")
        self.qld = mss2(NI, 4, "p9qld")
        self.sld = mss2(NI, 4, "p9sld")
        self.vsec = mss2(NI, 4, "p9vsec")
        self.tsec = mss2(NI, 4, "p9tsec")
        self.zpv = mss(NI, "p9zpv")
        self.smm = mss(NI, "p9smm")
        self.bkf = mss(NI, "p9bkf")
        self.g1m = mss(NI, "p9g1m")
        self.sxld = mss(NI, "p9sxld")
        self.cmb = mss(NI, "p9cmb")
        self.st = mss(NI, "p9st")
        self.dbgm = mss(4, "p9dbg")
        self.sl_dbg = ctx.slot("p9dbg")
        self.done = Milestone("p9done")

    def prev_done(self):
        i = self.ctx.phases.index(self)
        return None if i == 0 else self.ctx.phases[i - 1].done

    def sync(self, E):
        E.wait(self.prev_done())
        E.dma(self.dsc, self.dsc_d.ap(), slot=self.sl_misc)
        E.dma(self.dzb, self.dz_d.ap(), slot=self.sl_misc)
        for kt in range(IKT):
            E.dma(self.e86[:, kt * 128:(kt + 1) * 128], self.e86_d.ap()[kt],
                  ms=self.miscld if kt == IKT - 1 else None, slot=self.sl_misc)
        for tb in range(TB):
            if tb >= 1:
                E.wait(self.tsec[tb * KT - 1][3])
            for kt in range(IKT):
                E.dma(self.inter[:, kt * 512:(kt + 1) * 512],
                      self.int_d.ap()[kt * 128:(kt + 1) * 128, tb * 512:(tb + 1) * 512],
                      ms=self.ild[tb][kt], slot=self.sl_int)
            for nt in range(KT):
                i = tb * KT + nt
                for sec in range(4):
                    j = i * 4 + sec
                    if j >= 2:
                        pj = j - 2
                        E.wait(self.vsec[pj // 4][pj % 4])
                    b0, b1 = P9SEC[sec]
                    nb = b1 - b0
                    E.dma(self.qsb[j % 2][:, 0:nb * 128],
                          self.dq_d.ap()[nt].rearrange("p b c -> p (b c)")[:, b0 * 128:b1 * 128],
                          ms=self.qld[i][sec], slot=self.sl_q[j % 2])
                    E.dma(self.ssb[j % 2][:, 0:nb * 128],
                          self.ds_d.ap()[nt].rearrange("p b c -> p (b c)")[:, b0 * 128:b1 * 128],
                          ms=self.sld[i][sec], slot=self.sl_s[j % 2])
        E.wait(self.st[TB * KT - 1])
        E.set(E.nop(), self.done)

    def gpsimd(self, E):
        E.wait(self.miscld)

        def store(i):
            tb, nt = divmod(i, KT)
            E.wait(self.cmb[i])
            E.dma(self.out_d.ap()[nt * 128:(nt + 1) * 128, tb * 512:(tb + 1) * 512],
                  self.ost[i % 2], ms=self.st[i], slot=self.sl_st[i % 2])
        for i in range(TB * KT):
            tb, nt = divmod(i, KT)
            if i >= 2:
                E.wait(self.cmb[i - 2])
            E.dma(self.sch[i % 2],
                  self.s_d.ap()[nt * 128:(nt + 1) * 128, tb * 512:(tb + 1) * 512],
                  slot=self.sl_sx[i % 2])
            E.dma(self.xch[i % 2],
                  self.x1_d.ap()[nt * 128:(nt + 1) * 128, tb * 512:(tb + 1) * 512],
                  ms=self.sxld[i], slot=self.sl_sx[i % 2])
            if i >= 1:
                store(i - 1)
            if self.dbg and i == 1:
                E.dma(self.wdbg_d.ap(), self.wbuf, ms=self.dbgm[0], slot=self.sl_dbg)
                E.dma(self.xdbg_d.ap(), self.xgn, ms=self.dbgm[1], slot=self.sl_dbg)
                E.dma(self.zdbg_d.ap(), self.zpt[0], ms=self.dbgm[2], slot=self.sl_dbg)
                E.dma(self.fdbg_d.ap(), self.f16g[0], ms=self.dbgm[3], slot=self.sl_dbg)
        store(TB * KT - 1)

    def vector(self, E):
        E.wait(self.miscld)
        for i in range(TB * KT):
            tb, nt = divmod(i, KT)
            if nt == 0:
                E.wait(self.xgm[tb])
                E.tensor_scalar(out=self.xgn, in0=self.xgn, scalar1=-1.0, scalar2=None,
                                op0=ALU.mult)
                E.setd(self.xneg[tb])
            for sec in range(4):
                j = i * 4 + sec
                E.wait(self.qld[i][sec])
                E.wait(self.sld[i][sec])
                if self.dbg and i == 1 and sec == 0:
                    E.wait(self.dbgm[0])
                if i >= 1:
                    if sec < 3:
                        E.wait(self.tsec[i - 1][sec + 1])
                    else:
                        E.wait(self.smm[i - 1])
                b0, b1 = P9SEC[sec]
                nb = b1 - b0
                qs = self.qsb[j % 2][:, 0:nb * 128]
                E.tensor_single_scalar(out=self.lo8[:, 0:nb * 128], in_=qs, scalar=15,
                                       op=ALU.bitwise_and)
                E.tensor_single_scalar(out=self.hi8[:, 0:nb * 128], in_=qs, scalar=4,
                                       op=ALU.logical_shift_right)
                E.dchain()
                wv = self.wbuf.rearrange("p (k w c) -> p k w c", w=2, c=128)
                ss = self.ssb[j % 2][:, 0:nb * 128].rearrange("p (b c) -> p b c", c=128)
                E.tensor_mul(out=wv[:, b0:b1, 0, :],
                             in0=self.lo8[:, 0:nb * 128].rearrange("p (b c) -> p b c", c=128),
                             in1=ss)
                E.tensor_mul(out=wv[:, b0:b1, 1, :],
                             in0=self.hi8[:, 0:nb * 128].rearrange("p (b c) -> p b c", c=128),
                             in1=ss)
                E.setd(self.vsec[i][sec])
            if self.dbg and i == 2:
                E.wait(self.dbgm[2])
            if i >= 2:
                E.wait(self.smm[i - 2])
            b, par = divmod(nt, 2)
            op = ALU.bitwise_and if par == 0 else ALU.logical_shift_right
            sc = 15 if par == 0 else 4
            E.tensor_single_scalar(out=self.znib, in_=self.dzb[:, b * 128:(b + 1) * 128],
                                   scalar=sc, op=op)
            E.dchain()
            E.scalar_tensor_tensor(out=self.zpt[i % 2], in0=self.znib, scalar=1.0,
                                   in1=self.dsc[:, nt * 128:(nt + 1) * 128],
                                   op0=ALU.add, op1=ALU.mult)
            E.setd(self.zpv[i])
            E.wait(self.g1m[i])
            E.wait(self.sxld[i])
            if i >= 2:
                E.wait(self.st[i - 2])
            E.tensor_mul(out=self.ost[i % 2], in0=self.g1s[i % 2], in1=self.sch[i % 2])
            E.dchain()
            E.tensor_add(out=self.ost[i % 2], in0=self.ost[i % 2], in1=self.xch[i % 2])
            E.setd(self.cmb[i])

    def scalar(self, E):
        c = self.ctx
        for tb in range(TB):
            E.wait(self.xgmm[tb])
            if self.dbg and tb == 1:
                E.wait(self.dbgm[1])
            E.activation(out=self.xgn, in_=c.p_banks[4][:], func=AF.Copy)
            E.setd(self.xgm[tb])
            for nt in range(KT):
                i = tb * KT + nt
                E.wait(self.smm[i])
                if self.dbg and i == 2:
                    E.wait(self.dbgm[3])
                bank = c.sc4[:, (i % 2) * 512:(i % 2) * 512 + 512]
                E.activation(out=self.f16g[i % 2], in_=bank, func=AF.Copy)
                E.setd(self.bkf[i])
                if i >= 2:
                    E.wait(self.cmb[i - 2])
                E.activation(out=self.g1s[i % 2], in_=self.f16g[i % 2], func=AF.Silu)
                E.setd(self.g1m[i])

    def tensor(self, E):
        c = self.ctx
        E.wait(self.prev_done())
        E.wait(self.miscld)
        for tb in range(TB):
            if tb >= 1:
                E.wait(self.xgm[tb - 1])
            E.wait(self.ild[tb][IKT - 1])
            for kt in range(IKT):
                ins = E.matmul(out=c.p_banks[4][:],
                               lhsT=self.e86[:, kt * 128:(kt + 1) * 128],
                               rhs=self.inter[:, kt * 512:(kt + 1) * 512],
                               start=(kt == 0), stop=(kt == IKT - 1))
            E.set(ins, self.xgmm[tb])
            for nt in range(KT):
                i = tb * KT + nt
                bank = c.sc4[:, (i % 2) * 512:(i % 2) * 512 + 512]
                if i >= 2:
                    E.wait(self.bkf[i - 2])
                for sec in range(4):
                    E.wait(self.vsec[i][sec])
                    b0, b1 = P9SEC[sec]
                    for kt in range(2 * b0, 2 * b1):
                        ins = E.matmul(out=bank,
                                       lhsT=self.wbuf[:, kt * 128:(kt + 1) * 128],
                                       rhs=self.inter[:, kt * 512:(kt + 1) * 512],
                                       start=(kt == 0),
                                       stop=(self.nozp and kt == 85))
                    E.set(ins, self.tsec[i][sec])
                E.wait(self.zpv[i])
                E.wait(self.xneg[tb])
                if self.nozp:
                    ins = E.transpose(out=c.pt_banks[0][:, 0:128], in_=self.e86[:, 0:128],
                                      identity=c.consts.ident[:])
                    E.set(ins, self.smm[i])
                else:
                    ins = E.matmul(out=bank, lhsT=self.zpt[i % 2],
                                   rhs=self.xgn, start=False, stop=True)
                    E.set(ins, self.smm[i])

    @staticmethod
    def host_inputs(dq, dsc, dz):
        dq = np.asarray(dq)
        B = dq.view(np.uint8).reshape(INTER // 8, D, 4).transpose(0, 2, 1).reshape(INTER // 2, D)
        B = B[:, PH].reshape(43, 128, KT, 128).transpose(2, 1, 0, 3)
        scf = np.asarray(dsc)
        srep = np.repeat(scf, 64, axis=0)[:, PH]
        S = srep.reshape(43, 128, KT, 128).transpose(2, 1, 0, 3)
        e86 = np.zeros((IKT, 128, 128), np.float16)
        for t in range(IKT):
            g0 = 2 * (t // 2)
            e86[t, 0:64, g0] = 1
            e86[t, 64:128, g0 + 1] = 1
        dzp = np.zeros((128, 2048), np.uint8)
        dzp[0:86] = np.asarray(dz).view(np.uint8).reshape(86, 2048)
        dscp = np.zeros((128, 4096), np.float16)
        dscp[0:86] = scf[:, PH]
        return {"dq_b": np.ascontiguousarray(B), "ds_b": np.ascontiguousarray(S),
                "dz_b": dzp, "dscP": dscp, "e86": e86}


_PROG_CACHE = {}
import os as _os
_KDUMP = bool(_os.environ.get("KDUMP"))


def _prog(key, builder):
    if key not in _PROG_CACHE:
        _PROG_CACHE[key] = builder()
    return _PROG_CACHE[key]


def _blk(W):
    return np.ascontiguousarray(W.reshape(32, 128, 32, 128).transpose(2, 0, 1, 3))


def _lora_masks():
    import jax
    with jax.default_device(jax.devices("cpu")[0]):
        u = jax.random.uniform(jax.random.key(42), (8, T, D))
        u = np.asarray(u)
    return u


def _l0_weight_inputs(inputs):
    PHl = PH
    ins = {}
    for nm, key in (("wq_b", "L0_attn_w_q"), ("wk_b", "L0_attn_w_k"), ("wv_b", "L0_attn_w_v")):
        W = np.asarray(inputs[key], np.float32)
        ins[nm] = _blk(np.ascontiguousarray(W.T[PHl]))
    W0 = np.asarray(inputs["L0_ffn_W0"], np.float32)
    ins["w0_b"] = _blk(np.ascontiguousarray(W0.T[PHl][:, PHl]))
    A = np.asarray(inputs["L0_ffn_A"], np.float32)
    B = np.asarray(inputs["L0_ffn_B"], np.float32)
    atp = np.zeros((D, 128), np.float32)
    for l in range(4):
        atp[:, 32 * l:32 * l + 16] = A[l].T
    ins["at_b"] = np.ascontiguousarray(atp[PHl].reshape(32, 128, 128))
    bfull = np.zeros((32, 32, 512), np.float32)
    for l in range(4):
        Bl = B[l][PHl]
        for ot in range(32):
            bfull[ot, 0:16, 128 * l:128 * (l + 1)] = Bl[128 * ot:128 * (ot + 1)].T
    ins["bt_b"] = bfull
    b0 = np.asarray(inputs["L0_ffn_b0"], np.float32)
    ins["b0_t"] = np.ascontiguousarray(b0[PHl].reshape(32, 128).T)
    wo = np.asarray(inputs["L0_attn_w_o"], np.float32)
    ins["wo_b"] = _blk(np.ascontiguousarray(wo[PHl].T))
    return ins


def _m8_for_core(u_b):
    m8 = np.zeros((32, 4, 4, 128, 512), BF)
    for l in range(4):
        mask = ((u_b >= 0.25 * l) & (u_b < 0.25 * (l + 1))).astype(np.float32) * 8.0
        mT = mask.T[PH]
        m8[:, :, l] = mT.reshape(32, 128, 4, 512).transpose(0, 2, 1, 3).astype(BF)
    return m8


def _l1_weight_inputs(inputs):
    ins = {}
    w_down = np.asarray(inputs["L1_attn_w_down"], np.float32)
    ins["wdn_b"] = np.ascontiguousarray(w_down.T[PH].reshape(32, 128, 128))
    ins["bdn"] = np.asarray(inputs["L1_attn_b_down"], np.float32).reshape(128, 1)
    e32 = np.zeros((KT, 128, 32), np.float16)
    for t in range(KT):
        g0 = 2 * (t // 2)
        e32[t, 0:64, g0] = 1
        e32[t, 64:128, g0 + 1] = 1
    ins["e32"] = e32

    def qblock(qw):
        Bq = qw.view(np.uint8).reshape(512, INTER, 4).transpose(0, 2, 1).reshape(2048, INTER)
        Bq = Bq[:, PI]
        return np.ascontiguousarray(Bq.reshape(16, 128, 86, 128).transpose(2, 1, 0, 3))

    def sblock(sc):
        srep = np.repeat(sc, 64, axis=0)[:, PI]
        return np.ascontiguousarray(srep.reshape(16, 128, 86, 128).transpose(2, 1, 0, 3))
    gq = np.asarray(inputs["L1_ffn_gate_qweight"])
    uq = np.asarray(inputs["L1_ffn_up_qweight"])
    gs = np.asarray(inputs["L1_ffn_gate_scales"])
    us = np.asarray(inputs["L1_ffn_up_scales"])
    gz = np.asarray(inputs["L1_ffn_gate_zeros"])
    uz = np.asarray(inputs["L1_ffn_up_zeros"])
    ins["gq_b"] = qblock(gq); ins["uq_b"] = qblock(uq)
    ins["gs_b"] = sblock(gs); ins["us_b"] = sblock(us)
    ins["gz_b"] = gz.view(np.uint8).reshape(32, INTER // 2)
    ins["uz_b"] = uz.view(np.uint8).reshape(32, INTER // 2)
    ins["gsc"] = np.ascontiguousarray(gs[:, PI])
    ins["usc"] = np.ascontiguousarray(us[:, PI])
    return ins


def kernel(**inputs):
    x = np.asarray(inputs["L0_attn_x"], np.float32)
    cores = list(range(NCORES))

    io1 = {"x_tok": "in", "qT": "out", "kT": "out", "vT": "out", "g0T": "out"}
    nc1 = _prog("L1", lambda: build_program(
        [(P_LN, dict()), (P2_QKVG, dict())], io_kinds=io1, n_cores=NCORES))
    consts = SharedConsts.host_inputs()
    wl0 = _l0_weight_inputs(inputs)
    u_mask = _lora_masks()
    maps1 = []
    for b in cores:
        m = {**consts,
             "x_tok": np.ascontiguousarray(x[b][:, PH]),
             "wq_b": wl0["wq_b"], "wk_b": wl0["wk_b"], "wv_b": wl0["wv_b"],
             "w0_b": wl0["w0_b"], "at_b": wl0["at_b"], "bt_b": wl0["bt_b"],
             "b0_t": wl0["b0_t"], "m8_b": _m8_for_core(u_mask[b])}
        maps1.append(m)
    r1 = run_bass_kernel_spmd(nc1, maps1, core_ids=cores).results

    io2 = {"qT": "in", "kT": "in", "vT": "in", "g0T": "in", "wo_b": "in",
           "xfT": "in", "x1T": "out"}
    nc2 = _prog("L2", lambda: build_program(
        [(P3_ATTN, dict()), (P4_OPROJ, dict())], io_kinds=io2, n_cores=NCORES))
    maps2 = []
    for b in cores:
        maps2.append({**consts,
                      "qT": np.asarray(r1[b]["qT"]), "kT": np.asarray(r1[b]["kT"]),
                      "vT": np.asarray(r1[b]["vT"]), "g0T": np.asarray(r1[b]["g0T"]),
                      "wo_b": wl0["wo_b"],
                      "xfT": np.ascontiguousarray(x[b].T[PH])})
    r2 = run_bass_kernel_spmd(nc2, maps2, core_ids=cores).results
    if _KDUMP:
        np.save("/tmp/k_g0T.npy", np.asarray(r1[0]["g0T"]))
        np.save("/tmp/k_qT.npy", np.asarray(r1[0]["qT"]))
        np.save("/tmp/k_x1T.npy", np.asarray(r2[0]["x1T"]))

    io3 = {"x1T": "in", "hgeluT": "out", "interT": "out", "xn1T": "out"}
    nc3 = _prog("L3", lambda: build_program(
        [(P_LN, dict(src_fm=True, prefix="ln1", xname="x1T", out_dram="xn1T",
                     out_f16=True)),
         (P6_FFN1, dict())], io_kinds=io3, n_cores=NCORES))
    wl1 = _l1_weight_inputs(inputs)
    maps3 = []
    for b in cores:
        maps3.append({**consts, "x1T": np.asarray(r2[b]["x1T"]), **wl1})
    r3 = run_bass_kernel_spmd(nc3, maps3, core_ids=cores).results
    if _KDUMP:
        np.save("/tmp/k_hgeluT.npy", np.asarray(r3[0]["hgeluT"]).astype(np.float32))
        np.save("/tmp/k_interT.npy", np.asarray(r3[0]["interT"]).astype(np.float32))
        np.save("/tmp/k_xn1T.npy", np.asarray(r3[0]["xn1T"]).astype(np.float32))
        np.save("/tmp/k_hg_all.npy", np.stack([np.asarray(r3[b]["hgeluT"]).astype(np.float32) for b in cores]))

    hg = [np.asarray(r3[b]["hgeluT"]) for b in cores]
    io4 = {"hyuT": "in", "h4T": "out"}
    nc4 = _prog("L4", lambda: build_program(
        [(P7_HYDRA, dict())], io_kinds=io4, n_cores=NCORES))
    hyp = P7_HYDRA.host_params(
        inputs["L1_attn_hy_in_w"], inputs["L1_attn_hy_conv_w"],
        inputs["L1_attn_hy_conv_b"], inputs["L1_attn_hy_dt_bias"],
        inputs["L1_attn_hy_A_log"], inputs["L1_attn_hy_D"],
        inputs["L1_attn_hy_fcD_w"], inputs["L1_attn_hy_norm_w"],
        inputs["L1_attn_hy_out_w"], np.asarray(inputs["L1_attn_gate1"], np.float32))
    maps4 = []
    for i in cores:
        hyu = np.empty((128, T), np.float16)
        for l in range(8):
            hyu[:, l * 256:(l + 1) * 256] = hg[l][:, 256 * i:256 * (i + 1)]
        maps4.append({**consts, "hyuT": hyu, **hyp})
    r4 = run_bass_kernel_spmd(nc4, maps4, core_ids=cores).results
    if _KDUMP:
        np.save("/tmp/k_h4T.npy", np.asarray(r4[0]["h4T"]).astype(np.float32))
        np.save("/tmp/k_h4_all.npy", np.stack([np.asarray(r4[i]["h4T"]).astype(np.float32) for i in cores]))

    h4 = [np.asarray(r4[i]["h4T"]) for i in cores]
    io5 = {"h5T": "in", "xn1T": "in", "wup_b": "in", "bupT": "in",
           "interT": "in", "x1T": "in", "dq_b": "in", "ds_b": "in",
           "dz_b": "in", "dscP": "in", "e86": "in", "outT": "out"}
    nc5 = _prog("L5", lambda: build_program(
        [(P8_UP, dict()), (P9_FFN2, dict())], io_kinds=io5, n_cores=NCORES))
    up_ins = P8_UP.host_inputs(inputs["L1_attn_w_up"], inputs["L1_attn_b_up"])
    dn_ins = P9_FFN2.host_inputs(inputs["L1_ffn_down_qweight"],
                                 inputs["L1_ffn_down_scales"],
                                 inputs["L1_ffn_down_zeros"])
    maps5 = []
    for b in cores:
        h4all = np.stack(h4)
        h5 = h4all[:, :, b * 256:(b + 1) * 256].transpose(1, 0, 2).reshape(128, T)
        maps5.append({**consts, "h5T": np.ascontiguousarray(h5),
                      "xn1T": np.asarray(r3[b]["xn1T"]),
                      "interT": np.asarray(r3[b]["interT"]),
                      "x1T": np.asarray(r2[b]["x1T"]),
                      **up_ins, **dn_ins})
    r5 = run_bass_kernel_spmd(nc5, maps5, core_ids=cores).results

    out = np.empty((NCORES, T, D), np.float32)
    for b in cores:
        oT = np.asarray(r5[b]["outT"])
        out[b][:, PH] = oT.T
    return out
